# version 29
# baseline (speedup 1.0000x reference)
import numpy as np
import ml_dtypes
import concourse.bass as bass
import concourse.mybir as mybir
from concourse.bass_utils import run_bass_kernel_spmd
from contextlib import ExitStack

F32 = mybir.dt.float32
BF16 = mybir.dt.bfloat16
AF = mybir.ActivationFunctionType
ALU = mybir.AluOpType
AX = mybir.AxisListType

BF = ml_dtypes.bfloat16
P = 128
T = 2048
D = 1024
H = 16
NCH = 8
TCH = 16
NT = 4
SCALE = 0.125
EPS = 1e-5


def build_nc():
    nc = bass.Bass("TRN2", target_bir_lowering=False)

    x_h = nc.dram_tensor("x", [T, D], F32, kind="ExternalInput")
    out_h = nc.dram_tensor("out", [T, D], F32, kind="ExternalOutput")
    w_hs = {}
    for nm in ("Wq", "Wk", "Wt", "FW", "A1", "A2"):
        w_hs[nm] = nc.dram_tensor(nm, [D, D], F32, kind="ExternalInput")
    wqa_h = nc.dram_tensor("WqaL", [P, P], F32, kind="ExternalInput")
    wka_h = nc.dram_tensor("WkaL", [P, P], F32, kind="ExternalInput")
    bq_h = nc.dram_tensor("bqL", [P, 8], F32, kind="ExternalInput")
    bk_h = nc.dram_tensor("bkL", [P, 8], F32, kind="ExternalInput")
    bqa_h = nc.dram_tensor("bqaL", [H, 1], F32, kind="ExternalInput")
    bka_h = nc.dram_tensor("bkaL", [H, 1], F32, kind="ExternalInput")
    bt_h = nc.dram_tensor("btL", [1, D], F32, kind="ExternalInput")
    bb_h = nc.dram_tensor("bbL", [1, D], F32, kind="ExternalInput")
    selp_h = nc.dram_tensor("selp", [H, 1024], BF16, kind="ExternalInput")
    a16_h = nc.dram_tensor("a16", [H, P], BF16, kind="ExternalInput")
    maskb_h = nc.dram_tensor("maskb", [H, 8], F32, kind="ExternalInput")
    eye_h = nc.dram_tensor("eye", [P, P], BF16, kind="ExternalInput")
    ones2_h = nc.dram_tensor("ones2", [2, P], BF16, kind="ExternalInput")
    slot1_h = nc.dram_tensor("slot1", [P, 8], F32, kind="ExternalInput")
    peer1_h = nc.dram_tensor("peer1", [P, 8], F32, kind="ExternalInput")
    ccqi_h = nc.dram_tensor("ccqi", [P, 80], F32)
    ccqo_h = nc.dram_tensor("ccqo", [P, 80], F32)
    ccki_h = nc.dram_tensor("ccki", [P, 80], F32)
    ccko_h = nc.dram_tensor("ccko", [P, 80], F32)
    bthl_hbm = nc.dram_tensor("bthl_hbm", [2, D], BF16)
    bbhl_hbm = nc.dram_tensor("bbhl_hbm", [2, D], BF16)

    es = ExitStack()
    sb = lambda nm, shp, dt: es.enter_context(nc.sbuf_tensor(nm, shp, dt))
    psm = lambda nm, shp: es.enter_context(nc.psum_tensor(nm, shp, F32))

    B1 = sb("B1", [P, 16384], BF16)
    B2 = sb("B2", [P, 16384], BF16)
    B3 = sb("B3", [P, 16384], BF16)
    B4 = sb("B4", [P, 16384], BF16)
    H1B = sb("H1B", [P, 4096], BF16)
    WS0 = sb("WS0", [P, 8192], BF16)
    WS1 = sb("WS1", [P, 8192], BF16)
    WS1F = WS1.bitcast(F32)
    XBB = sb("XBB", [P, 2 * D], F32)
    OSB = sb("OSB", [P, 2 * D], F32)
    OS = [OSB[:, 0:D], OSB[:, D:2 * D]]
    U1 = sb("U1", [P, 512], F32)
    T2 = [sb(f"T2{i}", [P, 512], F32) for i in range(2)]
    SCRL = sb("SCRL", [P, 512], BF16)
    EB = sb("EB", [H, T], BF16)
    BNS = sb("BNS", [P, 48], F32)
    MV = sb("MV", [P, 2 * TCH], F32)
    NMRS4 = sb("NMRS4", [P, 4], F32)
    RSTD = sb("RSTD", [P, TCH], F32)
    M2S = sb("M2S", [P, 1], F32)
    NMRS = sb("NMRS", [P, 1], F32)
    WQAB = sb("WQAB", [P, P], BF16)
    WKAF = sb("WKAF", [P, P], F32)
    WKAB = sb("WKAB", [P, P], BF16)
    BQ = sb("BQ", [P, 8], F32)
    BK = sb("BK", [P, 8], F32)
    BQA = sb("BQA", [H, 1], F32)
    BKA = sb("BKA", [H, 1], F32)
    BQAS = sb("BQAS", [H, 1], F32)
    BKAS = sb("BKAS", [H, 1], F32)
    BTHL = sb("BTHL", [2, D], BF16)
    BBHL = sb("BBHL", [2, D], BF16)
    SELP = sb("SELP", [H, 1024], BF16)
    A16 = sb("A16", [H, P], BF16)
    MASKB = sb("MASKB", [H, 8], F32)
    EYE = sb("EYE", [P, P], BF16)
    ONES2 = sb("ONES2", [2, P], BF16)
    SLOT1 = sb("SLOT1", [P, 8], F32)
    PEER1 = sb("PEER1", [P, 8], F32)
    XCHQ = sb("XCHQ", [P, 10], F32)
    XCHK = sb("XCHK", [P, 10], F32)
    BIG = sb("BIG", [P, 80], F32)
    G = sb("G", [P, 80], F32)
    PEERQ = sb("PEERQ", [P, 10], F32)
    PEERK = sb("PEERK", [P, 10], F32)
    NEGM = sb("NEGM", [H, 1], F32)
    MRG = sb("MRG", [H, 16], F32)
    BAB = sb("BAB", [H, 16], BF16)
    PM1 = sb("PM1", [P, 8], F32)
    PM2 = sb("PM2", [P, 8], F32)
    PQ = sb("PQ", [P, 8], F32)
    PMK = sb("PMK", [P, 8], F32)
    PK = sb("PK", [P, 8], F32)
    ACC4 = sb("ACC4", [P, 4], F32)
    CSTS = sb("CSTS", [P, 4], F32)
    TP4 = sb("TP4", [P, 40], F32)

    psA = [psm(f"psA{i}", [P, 512]) for i in range(4)]
    psB = psm("psB", [P, T])

    sems = {}
    for nm in ("cst", "bqk", "wq", "wk", "fw", "a1", "a2", "wt", "bto", "bd", "b32",
               "bnq", "bnk", "gq", "gk", "x0", "x1", "x2", "x3", "x2e", "x2o", "oe", "oo",
               "tr0", "tr1", "tr2", "tr3", "sv", "ss",
               "mmA", "evA", "mmB", "evB", "sccq", "scck"):
        sems[nm] = es.enter_context(nc.semaphore(nm))

    cps = {}

    def walk(active, eng, dry):
        cnt = {k: 0 for k in sems}

        def bump(sem, k):
            cnt[sem] += k
            return cnt[sem]

        def CP(name):
            if dry:
                cps[name] = dict(cnt)

        def W(tag, sem, val):
            if not dry and active == tag and val > 0:
                eng.wait_ge(sems[sem], val)

        def WCP(tag, name, sem):
            if not dry:
                W(tag, sem, cps[name][sem])

        def VW():
            W("v", "sv", cnt["sv"])

        def E(tag, f, inc=None, inck=1):
            if not dry and active == tag:
                ins = f(eng)
                if inc is not None:
                    ins.then_inc(sems[inc], inck)
            if inc is not None:
                bump(inc, inck)

        SCR3 = BIG
        XBv = lambda i, a, b: XBB[:, (i % 2) * D + a: (i % 2) * D + b]
        QSv = XBB[0:16, :]
        XN = lambda i, a, b: B1[:, i * 1024 + a: i * 1024 + b]
        H2V = lambda c, a, b: B1[:, c * 2048 + a: c * 2048 + b]
        XNT = lambda c, a, b: B2[:, c * 2048 + a: c * 2048 + b]
        WTV = lambda k, a, b: B2[:, k * 1024 + a: k * 1024 + b]
        MQ = lambda c, a, b: B3[:, c * 2048 + a: c * 2048 + b]
        MK = lambda c, a, b: B4[:, c * 2048 + a: c * 2048 + b]
        WV = lambda c, a, b: B4[:, c * 2048 + a: c * 2048 + b]
        H1V = lambda k, a, b: H1B[:, k * 512 + a: k * 512 + b]
        WSL = lambda ws, k, a, b: ws[:, k * 1024 + a: k * 1024 + b]

        def x_pair_load(p):
            if p >= 2:
                WCP("g", f"xn{2 * (p - 2) + 1}", "sv")
            dst = bass.AP(H1B, (p % 2) * 2 * D, [[4 * D, P], [D, 2], [1, D]])
            src_ = bass.AP(x_h, 256 * p * D, [[D, P], [P * D, 2], [1, D]])
            E("g", lambda g, dst=dst, src_=src_: g.dma_start(dst, src_),
              f"x{p % 2}", 16)


        x_pair_load(0)
        x_pair_load(1)
        E("g", lambda g: g.dma_start(BQ[:, :], bq_h[:, :]), "bqk", 16)
        E("g", lambda g: g.dma_start(BK[:, :], bk_h[:, :]), "bqk", 16)

        def wload(name, sem, dst_handle, dst_row_elems, gate=None):
            if gate is not None:
                W("g", gate[0], gate[1])
            src = bass.AP(w_hs[name], 0, [[1024, 128], [131072, 8], [1, 1024]])
            dst = bass.AP(dst_handle, 0, [[dst_row_elems, 128], [1024, 8], [1, 1024]])
            E("g", lambda g: g.dma_start(dst, src), sem, 16)

        wload("Wq", "wq", WS0, 8192)
        x_pair_load(2)
        x_pair_load(3)
        for (dst, src) in ((SELP[:, :], selp_h[:, :]), (A16[:, :], a16_h[:, :]),
                           (MASKB[:, :], maskb_h[:, :]), (EYE[:, :], eye_h[:, :]),
                           (ONES2[:, :], ones2_h[:, :]), (SLOT1[:, :], slot1_h[:, :]),
                           (PEER1[:, :], peer1_h[:, :]), (BQA[:, :], bqa_h[:, :]),
                           (BKA[:, :], bka_h[:, :]), (WQAB[:, :], wqa_h[:, :]),
                           (WKAF[:, :], wka_h[:, :])):
            E("g", lambda g, d=dst, s=src: g.dma_start(d, s), "cst", 16)
        CP("consts")
        x_pair_load(4)
        x_pair_load(5)
        x_pair_load(6)
        x_pair_load(7)
        wload("Wk", "wk", WS1, 8192, gate=("evA", 4))
        def g_exchange(cci, cco, ccsem, bnsem, gsem, cpname):
            WCP("g", cpname + "big", "sv")
            E("g", lambda g, cci=cci: g.dma_start(cci[:, :], BIG[:, :]), bnsem, 16)
            W("g", bnsem, 16)
            E("g", lambda g, cci=cci, cco=cco: g.collective_compute(
                "AllReduce", ALU.add, replica_groups=[list(range(8))],
                ins=[cci.ap().opt()], outs=[cco.ap().opt()]), ccsem, 1)
            W("g", ccsem, cnt[ccsem])
            E("g", lambda g, cco=cco, gsem=gsem: g.dma_start(G[:, :], cco[:, :]),
              gsem, 16)
            CP(cpname + "gath")

        WCP("g", "bt_pair", "sv")
        E("g", lambda g: g.dma_start(
            bass.AP(bthl_hbm, 0, [[1024, 2], [1, 1024]]),
            bass.AP(EB, 0, [[2048, 1], [1, 2048]])), "bto", 16)
        WCP("g", "prep", "sv")
        E("g", lambda g: g.dma_start(
            bass.AP(bbhl_hbm, 0, [[1024, 2], [1, 1024]]),
            bass.AP(EB, 0, [[2048, 1], [1, 2048]])), "bto", 16)
        W("g", "bto", 32)
        E("g", lambda g: g.dma_start(BTHL[0:2, :], bthl_hbm[:, :]), "bd", 16)
        E("g", lambda g: g.dma_start(BBHL[0:2, :], bbhl_hbm[:, :]), "bd", 16)
        CP("biasdma")
        wload("FW", "fw", WS0, 8192, gate=("evA", 32))
        g_exchange(ccqi_h, ccqo_h, "sccq", "bnq", "gq", "q_")
        wload("A1", "a1", WS1, 8192, gate=("mmA", 64))
        g_exchange(ccki_h, ccko_h, "scck", "bnk", "gk", "k_")
        wload("A2", "a2", WS0, 8192, gate=("mmA", 120))
        W("g", "mmA", 120)
        srcWT = bass.AP(w_hs["Wt"], 0, [[1024, 128], [131072, 8], [1, 1024]])
        dstWT = bass.AP(B2, 0, [[16384, 128], [1024, 8], [1, 1024]])
        E("g", lambda g: g.dma_start(dstWT, srcWT), "wt", 16)

        def x_transpose(i):
            WCP("sy", f"xn{i}", "sv")
            out_ap = bass.AP(B2, 128 * i, [[16384, P], [2048, 8], [1, P]])
            E("sy", lambda s, i=i, out_ap=out_ap: s.dma_start_transpose(
                out_ap, XN(i, 0, D)), f"tr{i // 4}", 16)

        for i in range(TCH):
            x_transpose(i)

        for k in range(TCH // 2):
            WCP("sy", f"fin_{2 * k + 1}", "sv")
            E("sy", lambda s, k=k: s.dma_start(
                out_h[128 * (2 * k + 1):128 * (2 * k + 2), :], OS[1]),
              "oo", 16)

        E("v", lambda v: v.memset(CSTS[:, 0:1], -0.5), "sv")
        E("v", lambda v: v.memset(CSTS[:, 1:2], EPS), "sv")
        E("v", lambda v: v.memset(CSTS[:, 2:3], SCALE), "sv")

        def XC(i, a, b):
            off = ((i // 2) % 2) * 2 * D + (i % 2) * D
            return H1B[:, off + a:off + b]

        def ln_stats_chunk(i):
            W("v", f"x{(i // 2) % 2}", 16 * (i // 4 + 1))
            E("v", lambda v, i=i: v.bn_stats(
                BNS[:, 12 * (i % 4):12 * (i % 4) + 6], XC(i, 0, 512)), "sv")
            E("v", lambda v, i=i: v.bn_stats(
                BNS[:, 12 * (i % 4) + 6:12 * (i % 4) + 12], XC(i, 512, 1024)),
              "sv")
            CP(f"bns{i}")
            if i % 4 == 3:
                g = i // 4
                VW()
                for k in range(4):
                    j = 4 * g + k
                    E("v", lambda v, j=j, k=k: v.bn_aggr(
                        MV[:, 2 * j:2 * j + 2], BNS[:, 12 * k:12 * k + 12]),
                      "sv")
                CP(f"st{g}")

        def ln_xn_group(g):
            WCP("v", f"rstd{g}", "ss")
            E("v", lambda v, g=g: v.scalar_tensor_tensor(
                NMRS4[:, :], bass.AP(MV, 8 * g, [[2 * TCH, P], [2, 4]]), -1.0,
                RSTD[:, 4 * g:4 * g + 4], ALU.mult, ALU.mult), "sv")
            VW()
            for k in range(4):
                i = 4 * g + k
                E("v", lambda v, i=i, k=k: v.tensor_scalar(
                    XN(i, 0, D), XC(i, 0, D), RSTD[:, i:i + 1],
                    NMRS4[:, k:k + 1], ALU.mult, ALU.add), "sv")
                CP(f"xn{i}")

        for g in range(4):
            for i in range(4 * g, 4 * g + 4):
                ln_stats_chunk(i)
            ln_xn_group(g)

        E("v", lambda v: v.memset(XCHQ[:, :], 0.0), "sv")
        E("v", lambda v: v.memset(XCHK[:, :], 0.0), "sv")
        WCP("v", "consts", "cst")
        E("v", lambda v: v.tensor_scalar(BQAS[:, :], BQA[:, :], SCALE, None,
                                         ALU.mult), "sv")
        E("v", lambda v: v.tensor_scalar(BKAS[:, :], BKA[:, :], SCALE, None,
                                         ALU.mult), "sv")
        WCP("v", "b32", "b32")
        E("v", lambda v: v.tensor_copy(EB[0:1, 0:D], OSB[0:1, 0:D]), "sv")
        VW()
        E("v", lambda v: v.tensor_sub(EB[0:1, D:2 * D], OSB[0:1, 0:D],
                                      EB[0:1, 0:D]), "sv")
        CP("bt_pair")
        W("v", "bto", 16)
        E("v", lambda v: v.tensor_copy(EB[0:1, 0:D], OSB[0:1, D:2 * D]), "sv")
        VW()
        E("v", lambda v: v.tensor_sub(EB[0:1, D:2 * D], OSB[0:1, D:2 * D],
                                      EB[0:1, 0:D]), "sv")
        CP("prep")

        SCX = lambda half, a, b: XBB[32:48, half * 1024 + a: half * 1024 + b]

        def softmax_part(xch, cpbase):
            WCP("v", cpbase + "s", "ss")
            E("v", lambda v: v.tensor_max(SCX(0, 0, 1024), QSv[:, 0:1024],
                                          QSv[:, 1024:2048]), "sv")
            w, half = 512, 1
            while w >= 1:
                oh, ih = half, 1 - half
                VW()
                E("v", lambda v, w=w, oh=oh, ih=ih: v.tensor_max(
                    SCX(oh, 0, w), SCX(ih, 0, w), SCX(ih, w, 2 * w)), "sv")
                w //= 2
                half = 1 - half
            mx = SCX(1 - half, 0, 1)
            VW()
            E("v", lambda v, mx=mx: v.tensor_scalar(NEGM[:, :], mx, -1.0,
                                                    None, ALU.mult), "sv")
            E("v", lambda v, mx=mx, xch=xch: v.tensor_copy(xch[0:16, 8:9], mx),
              "sv")
            CP(cpbase + "negm")
            WCP("v", cpbase + "eb", "ss")
            E("v", lambda v, xch=xch: v.scalar_tensor_tensor(
                QSv[:, :], EB[:, :], 1.0, EB[:, :], ALU.mult, ALU.max,
                accum_out=xch[0:16, 9:10]), "sv")
            W("v", "sv", cnt["sv"])
            CP(cpbase + "sum")

        def v_pool(xch, srcf, cpname):
            base_mmB = cps[cpname + "erep0"]["mmB"] if not dry else 0
            idx = 0
            for c in range(NCH):
                for n in range(NT):
                    if not dry:
                        W("v", "mmB", base_mmB + idx + 1)
                    sl = (base_mmB + idx) % 4 if not dry else 0
                    E("v", lambda v, c=c, n=n, sl=sl, srcf=srcf:
                      v.scalar_tensor_tensor(
                          SCRL[:, 0:512], srcf(c, 512 * n, 512 * n + 512), 1.0,
                          psB[:, 512 * sl:512 * sl + 512], ALU.mult, ALU.mult,
                          accum_out=ACC4[:, n:n + 1]), "evB")
                    idx += 1
                W("v", "evB", cnt["evB"])
                VW()
                E("v", lambda v: v.tensor_add(M2S[:, :], ACC4[:, 0:1],
                                              ACC4[:, 1:2]), "sv")
                E("v", lambda v: v.tensor_add(NMRS[:, :], ACC4[:, 2:3],
                                              ACC4[:, 3:4]), "sv")
                VW()
                E("v", lambda v, c=c, xch=xch: v.tensor_add(
                    xch[:, c:c + 1], M2S[:, :], NMRS[:, :]), "sv")
            CP(cpname + "pool")

        def v_big(xch, cpname):
            VW()
            E("v", lambda v, xch=xch: v.tensor_tensor(
                bass.AP(BIG, 0, [[80, P], [10, 8], [1, 10]]),
                bass.AP(xch, 0, [[10, P], [0, 8], [1, 10]]),
                bass.AP(SLOT1, 0, [[8, P], [1, 8], [0, 10]]), ALU.mult), "sv")
            CP(cpname + "big")

        def v_merge(xch, peer, cpname, dstP):
            W("v", "gq" if cpname == "q_" else "gk", 16)
            for r in range(8):
                E("v", lambda v, r=r: v.tensor_scalar(
                    SCR3[:, 10 * r:10 * r + 10], G[:, 10 * r:10 * r + 10],
                    PEER1[:, r:r + 1], None, ALU.mult), "sv")
            VW()
            for p_ in range(4):
                E("v", lambda v, p_=p_: v.tensor_add(
                    TP4[:, 10 * p_:10 * p_ + 10],
                    SCR3[:, 20 * p_:20 * p_ + 10],
                    SCR3[:, 20 * p_ + 10:20 * p_ + 20]), "sv")
            VW()
            E("v", lambda v: v.tensor_add(SCR3[:, 0:10], TP4[:, 0:10],
                                          TP4[:, 10:20]), "sv")
            E("v", lambda v: v.tensor_add(SCR3[:, 10:20], TP4[:, 20:30],
                                          TP4[:, 30:40]), "sv")
            VW()
            E("v", lambda v, peer=peer: v.tensor_add(
                peer[:, :], SCR3[:, 0:10], SCR3[:, 10:20]), "sv")
            VW()
            E("v", lambda v, xch=xch, peer=peer: v.tensor_tensor(
                MRG[:, 0:1], xch[0:16, 8:9], peer[0:16, 8:9], ALU.max), "sv")
            VW()
            E("v", lambda v: v.tensor_scalar(MRG[:, 1:2], MRG[:, 0:1], -1.0,
                                             None, ALU.mult), "sv")
            CP(cpname + "negM")
            WCP("v", cpname + "fab", "ss")
            E("v", lambda v, xch=xch: v.tensor_mul(MRG[:, 4:5], xch[0:16, 9:10],
                                                   MRG[:, 2:3]), "sv")
            E("v", lambda v, peer=peer: v.tensor_mul(MRG[:, 5:6], peer[0:16, 9:10],
                                                     MRG[:, 3:4]), "sv")
            VW()
            E("v", lambda v: v.tensor_add(MRG[:, 6:7], MRG[:, 4:5], MRG[:, 5:6]), "sv")
            VW()
            E("v", lambda v: v.reciprocal(MRG[:, 7:8], MRG[:, 6:7]), "sv")
            VW()
            E("v", lambda v: v.tensor_mul(MRG[:, 8:9], MRG[:, 2:3], MRG[:, 7:8]), "sv")
            E("v", lambda v: v.tensor_mul(MRG[:, 9:10], MRG[:, 3:4], MRG[:, 7:8]), "sv")
            VW()
            E("v", lambda v: v.tensor_tensor(
                BAB[:, 0:8], bass.AP(MRG, 8, [[16, 16], [0, 8]]), MASKB[:, :],
                ALU.mult), "sv")
            E("v", lambda v: v.tensor_tensor(
                BAB[:, 8:16], bass.AP(MRG, 9, [[16, 16], [0, 8]]), MASKB[:, :],
                ALU.mult), "sv")
            CP(cpname + "bab")
            if not dry:
                W("v", "mmB", cps[cpname + "rep"]["mmB"])
            sl = (cps[cpname + "rep"]["mmB"] - 1) % 4 if not dry else 0
            E("v", lambda v, xch=xch, sl=sl: v.tensor_mul(
                PM1[:, :], xch[:, 0:8], psB[:, 512 * sl:512 * sl + 8]), "sv")
            E("v", lambda v, peer=peer, sl=sl: v.tensor_mul(
                PM2[:, :], peer[:, 0:8], psB[:, 512 * sl + 8:512 * sl + 16]),
              "evB")
            VW()
            E("v", lambda v, dstP=dstP: v.tensor_add(dstP[:, :], PM1[:, :],
                                                     PM2[:, :]), "sv")
            CP(cpname + "P")

        softmax_part(XCHQ, "q_")
        v_pool(XCHQ, MQ, "q_")
        v_big(XCHQ, "q_")
        v_merge(XCHQ, PEERQ, "q_", PQ)
        VW()
        for c in range(NCH):
            E("v", lambda v, c=c: v.tensor_scalar(
                WKAB[:, c * 16:(c + 1) * 16], WKAF[:, c * 16:(c + 1) * 16],
                PQ[:, c:c + 1], None, ALU.mult), "sv")
        CP("wkab")
        softmax_part(XCHK, "k_")
        v_pool(XCHK, MK, "k_")
        v_big(XCHK, "k_")
        v_merge(XCHK, PEERK, "k_", PMK)
        VW()
        E("v", lambda v: v.tensor_mul(PK[:, :], PQ[:, :], PMK[:, :]), "sv")
        VW()
        for c in range(NCH):
            E("v", lambda v, c=c: v.tensor_scalar(
                WV(c, 0, 2048), MQ(c, 0, 2048), PK[:, c:c + 1], None, ALU.mult),
              "sv")
        CP("wv")


        def s_rstd_group(g):
            WCP("s", f"st{g}", "sv")
            var4 = bass.AP(MV, 8 * g + 1, [[2 * TCH, P], [2, 4]])
            E("s", lambda s, var4=var4: s.activation(var4, var4, AF.Ln,
                                                     bias=CSTS[:, 1:2]), "ss")
            E("s", lambda s, g=g, var4=var4: s.activation(
                RSTD[:, 4 * g:4 * g + 4], var4, AF.Exp, scale=CSTS[:, 0:1]),
              "ss")
            CP(f"rstd{g}")

        s_rstd_group(0)
        s_rstd_group(1)

        def s_late_ln(tid):
            if tid == 8:
                s_rstd_group(2)
            if tid == 16:
                s_rstd_group(3)
            if tid == 20:
                WCP("s", f"xn{TCH - 1}", "sv")
                E("s", lambda s: s.dma_start(OSB[0:1, 0:D], bt_h[:, :]),
                  "b32", 16)
                E("s", lambda s: s.dma_start(OSB[0:1, D:2 * D], bb_h[:, :]),
                  "b32", 16)
                CP("b32")

        def proj_tile(ws, m, movf, mov_a, dst_ap, bias_ap, pool=4):
            bi = cnt["mmA"] % pool
            bank = psA[bi] if bi < 4 else psB[:, 512 * (bi - 4):512 * (bi - 4) + 512]
            if cnt["mmA"] >= pool:
                W("t", "evA", cnt["mmA"] - (pool - 1))
            for k in range(NCH):
                E("t", lambda t, ws=ws, k=k, m=m, bank=bank:
                  t.matmul(bank[:, :] if not isinstance(bank, bass.AP) else bank,
                           WSL(ws, k, m * 128, (m + 1) * 128),
                           movf(k, mov_a, mov_a + 512),
                           start=(k == 0), stop=(k == NCH - 1)),
                  "mmA" if k == NCH - 1 else None)
            tid = cnt["mmA"]
            W("s", "mmA", tid)
            if tid == 1:
                W("s", "bqk", 32)
            s_late_ln(tid)
            bank_ap = bank if isinstance(bank, bass.AP) else bank[:, :]
            if bias_ap is None:
                E("s", lambda s, dst_ap=dst_ap, bank_ap=bank_ap:
                  s.activation(dst_ap, bank_ap, AF.Identity), "evA")
            else:
                E("s", lambda s, dst_ap=dst_ap, bank_ap=bank_ap, bias_ap=bias_ap:
                  s.activation(dst_ap, bank_ap, AF.Identity, bias=bias_ap),
                  "evA")

        W("t", "wq", 16)
        for wi in range(4):
            WCP("t", f"xn{wi}", "sv")
            for wj in range(10):
                E("t", lambda t: t.matmul(psB[:, 0:512], WS0[:, 0:128],
                                          WS0[:, 0:512], start=True, stop=True),
                  None)
        for wi in range(3):
            W("t", "tr0", 16 * (wi + 1))
            for wj in range(8):
                E("t", lambda t: t.matmul(psB[:, 0:512], WS0[:, 0:128],
                                          WS0[:, 0:512], start=True, stop=True),
                  None)

        for n in range(NT):
            W("t", f"tr{n}", 16 * 4)
            W("t", "wq", 16)
            for m in range(NCH):
                proj_tile(WS0, m, XNT, 512 * n, MQ(m, 512 * n, 512 * n + 512),
                          BQ[:, m:m + 1], pool=8)
        CP("step1")

        def mk_tiles(n_list, hook=None):
            W("t", "wk", 16)
            ti = 0
            for n in n_list:
                W("t", f"tr{n}", 16 * 4)
                for m in range(NCH):
                    proj_tile(WS1, m, XNT, 512 * n,
                              MK(m, 512 * n, 512 * n + 512), BK[:, m:m + 1])
                    if hook is not None:
                        hook(ti)
                    ti += 1

        def score_phase(stat_buf, movf, bias_sc, cpbase, ev_gate):
            for n in range(NT):
                if ev_gate is not None:
                    W("t", "evA", ev_gate(n))
                if cnt["mmB"] >= 4:
                    W("t", "evB", cnt["mmB"] - 3)
                for k in range(NCH):
                    E("t", lambda t, n=n, k=k, stat_buf=stat_buf:
                      t.matmul(psB[0:16, 512 * n:512 * n + 512],
                               stat_buf[:, k * 16:(k + 1) * 16],
                               movf(k, 512 * n, 512 * n + 512),
                               start=(k == 0), stop=(k == NCH - 1)),
                      "mmB" if k == NCH - 1 else None)
            W("s", "mmB", cnt["mmB"])
            WCP("s", "prep", "sv")
            WCP("s", f"xn{TCH - 1}", "sv")
            E("s", lambda s, bias_sc=bias_sc: s.activation(
                QSv, psB[0:16, :], AF.Identity, bias=bias_sc,
                scale=CSTS[0:16, 2:3]), "ss")
            E("s", lambda s: s.nop(), "evB", 4)
            CP(cpbase + "s")
            WCP("s", cpbase + "negm", "sv")
            if cpbase == "q_":
                W("s", "bd", 32)
            if cpbase == "k_":
                if not dry:
                    W("s", "mmB", cps["q_erep_done"]["mmB"])
            E("s", lambda s: s.activation(EB[:, :], QSv, AF.Exp,
                                          bias=NEGM[:, :]), "ss")
            CP(cpbase + "eb")

        def erep_group(cpname, c):
            if c == 0:
                WCP("t", cpname + "eb", "ss")
                CP(cpname + "erep0")
            for n in range(NT):
                sl = cnt["mmB"] % 4
                if cnt["mmB"] >= 4:
                    W("t", "evB", cnt["mmB"] - 3)
                E("t", lambda t, c=c, n=n, sl=sl:
                  t.matmul(psB[:, 512 * sl:512 * sl + 512],
                           SELP[:, c * 128:(c + 1) * 128],
                           EB[0:16, 512 * n:512 * n + 512],
                           start=True, stop=True), "mmB")
            if c == NCH - 1:
                CP(cpname + "erep_done")

        def rep_mms(cpname):
            WCP("t", cpname + "bab", "sv")
            sl = cnt["mmB"] % 4
            if cnt["mmB"] >= 4:
                W("t", "evB", cnt["mmB"] - 3)
            E("t", lambda t, sl=sl: t.matmul(psB[:, 512 * sl:512 * sl + 8],
                                             A16[:, :], BAB[:, 0:8],
                                             start=True, stop=True), None)
            E("t", lambda t, sl=sl: t.matmul(psB[:, 512 * sl + 8:512 * sl + 16],
                                             A16[:, :], BAB[:, 8:16],
                                             start=True, stop=True), "mmB")
            CP(cpname + "rep")

        def s_fab(cpname, xch, peer):
            WCP("s", cpname + "negM", "sv")
            E("s", lambda s, xch=xch: s.activation(MRG[:, 2:3], xch[0:16, 8:9],
                                                   AF.Exp, bias=MRG[0:16, 1:2]),
              "ss")
            E("s", lambda s, peer=peer: s.activation(MRG[:, 3:4], peer[0:16, 8:9],
                                                     AF.Exp, bias=MRG[0:16, 1:2]),
              "ss")
            CP(cpname + "fab")

        def h1_phase(n):
            W("t", "fw", 16)
            for m in range(NCH):
                proj_tile(WS0, m, XNT, 512 * n, H1V(m, 0, 512), None)

        def h2_phase(n, first, hook=None):
            W("t", "a1", 16)
            W("t", "evA", cnt["mmA"])
            for m in range(NCH):
                bank = psA[cnt["mmA"] % 4]
                if cnt["mmA"] >= 4:
                    W("t", "evA", cnt["mmA"] - 3)
                for k in range(NCH):
                    E("t", lambda t, k=k, m=m, bank=bank:
                      t.matmul(bank[:, :], WSL(WS1, k, m * 128, (m + 1) * 128),
                               H1V(k, 0, 512),
                               start=(k == 0), stop=(k == NCH - 1)),
                      "mmA" if k == NCH - 1 else None)
                tid = cnt["mmA"]
                W("s", "mmA", tid)
                if first and m == 0:
                    for bn in range(4):
                        W("s", f"tr{bn}", 16 * 4)
                E("s", lambda s, m=m, n=n, bank=bank: s.activation(
                    H2V(m, 512 * n, 512 * n + 512), bank[:, :], AF.Identity),
                  "evA")
                if hook is not None:
                    hook(m)

        W("t", "evA", 32)
        WCP("t", "consts", "cst")
        score_phase(WQAB, MQ, BQAS[:, :], "q_", None)
        mk_tiles([0])
        mk_tiles([1], hook=lambda ti: erep_group("q_", ti) if ti < 8 else None)
        mk_tiles([2, 3])
        h1_phase(0)
        s_fab("q_", XCHQ, PEERQ)
        rep_mms("q_")
        W("t", "evA", 64)
        WCP("t", "wkab", "sv")
        score_phase(WKAB, MK, BKAS[:, :], "k_", None)
        h2_phase(0, True)
        h1_phase(1)
        h2_phase(1, False,
                 hook=lambda m: erep_group("k_", m) if m < 8 else None)
        h1_phase(2)
        h2_phase(2, False)
        s_fab("k_", XCHK, PEERK)
        rep_mms("k_")
        h1_phase(3)
        h2_phase(3, False)
        CP("h2done")

        W("t", "bd", 32)
        bld = []
        for n in range(4):
            src_hl, j = (BTHL, BBHL)[n // 2], n % 2
            bank = psA[cnt["mmA"] % 4]
            if cnt["mmA"] >= 4:
                W("t", "evA", cnt["mmA"] - 3)
            E("t", lambda t, src_hl=src_hl, j=j, bank=bank:
              t.matmul(bank[:, :], ONES2[:, :], src_hl[:, 512 * j:512 * j + 512],
                       start=True, stop=True), "mmA")
            bld.append(cnt["mmA"])
        for n in range(4):
            bank = psA[(bld[n] - 1) % 4]
            W("v", "mmA", bld[n])
            E("v", lambda v, n=n, bank=bank: v.tensor_copy(
                WS1F[:, 512 * n:512 * n + 512], bank[:, :]), "sv")
            E("v", lambda v: v.nop(), "evA", 1)

        def x2_load(i):
            if i >= 2:
                WCP("g", f"fin_u1_{i - 2}", "sv")
            else:
                WCP("g", "k_eb", "ss")
                WCP("g", "k_sum", "sv")
            E("g", lambda g, i=i: g.dma_start(XBv(i, 0, D),
                                              x_h[128 * i:128 * (i + 1), :]),
              "x2e" if i % 2 == 0 else "x2o", 16)

        x2_load(0)
        x2_load(1)

        W("t", "a2", 16)
        W("t", "wt", 16)
        W("t", "bd", 32)
        WCP("t", "wv", "sv")
        for i in range(TCH):
            tids = []
            for j in range(2):
                bank = psA[cnt["mmA"] % 4]
                if cnt["mmA"] >= 4:
                    W("t", "evA", cnt["mmA"] - 3)
                for k in range(NCH):
                    E("t", lambda t, i=i, j=j, k=k, bank=bank:
                      t.matmul(bank[:, :], WV(k, 128 * i, 128 * i + 128),
                               WTV(k, 512 * j, 512 * j + 512),
                               start=(k == 0), stop=False), None)
                for q in range(4):
                    c = 4 * j + q
                    E("t", lambda t, i=i, c=c, q=q, bank=bank:
                      t.matmul(bank[:, 128 * q:128 * q + 128],
                               MQ(c, 128 * i, 128 * i + 128), EYE[:, :],
                               start=False, stop=(q == 3)),
                      "mmA" if q == 3 else None)
                tids.append(cnt["mmA"])
            for j in range(2):
                bank = psA[cnt["mmA"] % 4]
                if cnt["mmA"] >= 4:
                    W("t", "evA", cnt["mmA"] - 3)
                for k in range(NCH):
                    E("t", lambda t, i=i, j=j, k=k, bank=bank:
                      t.matmul(bank[:, :], H2V(k, 128 * i, 128 * i + 128),
                               WSL(WS0, k, 512 * j, 512 * j + 512),
                               start=(k == 0), stop=(k == NCH - 1)),
                      "mmA" if k == NCH - 1 else None)
                tids.append(cnt["mmA"])

            FBV = [WS1F[:, 2048 + 512 * jj:2560 + 512 * jj] for jj in range(2)]
            U1B = [WS1F[:, 3072 + 512 * jj:3584 + 512 * jj] for jj in range(2)]
            for j in range(2):
                W("v", "mmA", tids[j])
                if j == 0:
                    W("v", "x2e" if i % 2 == 0 else "x2o", 16 * (i // 2 + 1))
                    if i >= 2:
                        W("v", "oe" if i % 2 == 0 else "oo",
                          16 * ((i - 2) // 2 + 1))
                    if i == 0:
                        VW()
                E("v", lambda v, i=i, j=j, a_bank=psA[(tids[j] - 1) % 4]:
                  v.tensor_add(U1B[j], a_bank[:, :],
                               XBv(i, 512 * j, 512 * j + 512)), "sv")
                E("v", lambda v: v.nop(), "evA", 1)
            CP(f"fin_u1_{i}")
            for j in range(2):
                W("v", "mmA", tids[2 + j])
                E("v", lambda v, j=j, f_bank=psA[(tids[2 + j] - 1) % 4]:
                  v.tensor_add(FBV[j], f_bank[:, :],
                               WS1F[:, 1024 + 512 * j:1536 + 512 * j]), "sv")
                E("v", lambda v: v.nop(), "evA", 1)
                CP(f"fin_fb_{i}_{j}")

            for j in range(2):
                WCP("s", f"fin_fb_{i}_{j}", "sv")
                if i >= 1:
                    WCP("s", f"fin_og_{i - 1}_{j}", "sv")
                E("s", lambda s, j=j: s.activation(
                    T2[j][:, :], FBV[j][:, :], AF.Tanh), "ss")
                CP(f"fin_t2_{i}_{j}")

            VW()
            for j in range(2):
                E("v", lambda v, j=j: v.tensor_add(
                    U1B[j], U1B[j], WS1F[:, 512 * j:512 * j + 512]), "sv")
            for j in range(2):
                WCP("v", f"fin_t2_{i}_{j}", "ss")
                VW()
                E("v", lambda v, i=i, j=j: v.tensor_add(
                    OSB[:, (i % 2) * D + 512 * j:(i % 2) * D + 512 * j + 512],
                    U1B[j], T2[j][:, :]), "sv")
                CP(f"fin_og_{i}_{j}")
            CP(f"fin_{i}")

            if i % 2 == 0:
                WCP("s", f"fin_{i}", "sv")
                E("s", lambda s, i=i: s.dma_start(
                    out_h[128 * i:128 * (i + 1), :], OS[i % 2]), "oe", 16)
            if i + 2 < TCH:
                x2_load(i + 2)

        W("s", "oe", 16 * (TCH // 2))
        W("s", "oo", 16 * (TCH // 2))
        return cnt

    final_counts = walk("dry", None, True)
    for nm, snap in cps.items():
        for k, v in snap.items():
            assert v <= final_counts[k], (nm, k, v, final_counts[k])

    with nc.Block() as block:
        @block.gpsimd
        def _(g):
            walk("g", g, False)

        @block.sync
        def _(sy):
            walk("sy", sy, False)

        @block.vector
        def _(v):
            walk("v", v, False)

        @block.scalar
        def _(s):
            walk("s", s, False)

        @block.tensor
        def _(t):
            walk("t", t, False)

    es.close()
    return nc


def make_in_maps(g):
    x = np.asarray(g["L0_attn_hidden_states"], dtype=np.float32)
    Wq = np.asarray(g["L0_attn_Wq"], np.float32)
    Wk = np.asarray(g["L0_attn_Wk"], np.float32)
    Wt = np.asarray(g["L0_attn_Wt"], np.float32)
    FW = np.asarray(g["L0_ffn_weight"], np.float32)
    A1 = np.asarray(g["L0_ffn_aux_weight1"], np.float32)
    A2 = np.asarray(g["L0_ffn_aux_weight2"], np.float32)
    Wqa = np.asarray(g["L0_attn_Wqa"], np.float32)
    Wka = np.asarray(g["L0_attn_Wka"], np.float32)
    bq = np.asarray(g["L0_attn_bq"], np.float32)
    bk = np.asarray(g["L0_attn_bk"], np.float32)
    bqa = np.asarray(g["L0_attn_bqa"], np.float32)
    bka = np.asarray(g["L0_attn_bka"], np.float32)
    bt = np.asarray(g["L0_attn_bt"], np.float32)
    bb = np.asarray(g["L0_ffn_bias"], np.float32)

    C = np.ascontiguousarray
    wqa_l = C(Wqa.reshape(8, 128, 16).transpose(1, 0, 2).reshape(128, 128))
    wka_l = C(Wka.reshape(8, 128, 16).transpose(1, 0, 2).reshape(128, 128))
    bq_l = C(bq.reshape(8, 128).T)
    bk_l = C(bk.reshape(8, 128).T)
    bqa_l = C(bqa.reshape(16, 1))
    bka_l = C(bka.reshape(16, 1))
    bt_l = C(bt.reshape(1, 1024))
    bb_l = C(bb.reshape(1, 1024))

    selp = np.zeros((16, 8, 128), np.float32)
    for c in range(8):
        selp[2 * c, c, 0:64] = 1.0
        selp[2 * c + 1, c, 64:128] = 1.0
    selp = C(selp.reshape(16, 1024).astype(BF))
    a16 = np.zeros((16, 128), np.float32)
    a16[0::2, 0:64] = 1.0
    a16[1::2, 64:128] = 1.0
    a16 = C(a16.astype(BF))
    maskb = np.zeros((16, 8), np.float32)
    for h in range(16):
        maskb[h, h // 2] = 1.0
    eye = C(np.eye(128, dtype=np.float32).astype(BF))
    ones2 = C(np.ones((2, 128), np.float32).astype(BF))

    in_maps = []
    for c in range(8):
        b, hf = c // 2, c % 2
        slab = C(x[b, hf * 2048:(hf + 1) * 2048, :])
        slot1 = np.zeros((128, 8), np.float32); slot1[:, c] = 1.0
        peer1 = np.zeros((128, 8), np.float32); peer1[:, c ^ 1] = 1.0
        in_maps.append({
            "x": slab, "Wq": Wq, "Wk": Wk, "Wt": Wt, "FW": FW, "A1": A1,
            "A2": A2, "WqaL": wqa_l, "WkaL": wka_l, "bqL": bq_l, "bkL": bk_l,
            "bqaL": bqa_l, "bkaL": bka_l, "btL": bt_l, "bbL": bb_l,
            "selp": selp, "a16": a16, "maskb": maskb, "eye": eye,
            "ones2": ones2, "slot1": slot1, "peer1": peer1,
        })
    return in_maps


def kernel(**inputs):
    g = {k: np.asarray(v) for k, v in inputs.items()}
    in_maps = make_in_maps(g)
    nc = build_nc()
    res = run_bass_kernel_spmd(nc, in_maps, core_ids=list(range(8)))
    out = np.empty((4, 4096, 1024), np.float32)
    for c in range(8):
        b, hf = c // 2, c % 2
        out[b, hf * 2048:(hf + 1) * 2048, :] = res.results[c]["out"]
    return out


# version 31
# speedup vs baseline: 1.1216x; 1.1216x over previous
import numpy as np
import ml_dtypes
import concourse.bass as bass
import concourse.mybir as mybir
from concourse.bass_utils import run_bass_kernel_spmd
from contextlib import ExitStack

F32 = mybir.dt.float32
BF16 = mybir.dt.bfloat16
AF = mybir.ActivationFunctionType
ALU = mybir.AluOpType
AX = mybir.AxisListType

BF = ml_dtypes.bfloat16
P = 128
T = 2048
D = 1024
H = 16
NCH = 8
TCH = 16
NT = 4
SCALE = 0.125
EPS = 1e-5


def build_nc():
    nc = bass.Bass("TRN2", target_bir_lowering=False)

    x_h = nc.dram_tensor("x", [T, D], F32, kind="ExternalInput")
    out_h = nc.dram_tensor("out", [T, D], F32, kind="ExternalOutput")
    w_hs = {}
    for nm in ("Wq", "Wk", "Wt", "FW", "A1", "A2"):
        w_hs[nm] = nc.dram_tensor(nm, [D, D], F32, kind="ExternalInput")
    wqa_h = nc.dram_tensor("WqaL", [P, P], F32, kind="ExternalInput")
    wka_h = nc.dram_tensor("WkaL", [P, P], F32, kind="ExternalInput")
    bq_h = nc.dram_tensor("bqL", [P, 8], F32, kind="ExternalInput")
    bk_h = nc.dram_tensor("bkL", [P, 8], F32, kind="ExternalInput")
    bqa_h = nc.dram_tensor("bqaL", [H, 1], F32, kind="ExternalInput")
    bka_h = nc.dram_tensor("bkaL", [H, 1], F32, kind="ExternalInput")
    bt_h = nc.dram_tensor("btL", [1, D], F32, kind="ExternalInput")
    bb_h = nc.dram_tensor("bbL", [1, D], F32, kind="ExternalInput")
    selp_h = nc.dram_tensor("selp", [H, 1024], BF16, kind="ExternalInput")
    a16_h = nc.dram_tensor("a16", [H, P], BF16, kind="ExternalInput")
    maskb_h = nc.dram_tensor("maskb", [H, 8], F32, kind="ExternalInput")
    eye_h = nc.dram_tensor("eye", [P, P], BF16, kind="ExternalInput")
    ones2_h = nc.dram_tensor("ones2", [2, P], BF16, kind="ExternalInput")
    slot1_h = nc.dram_tensor("slot1", [P, 8], F32, kind="ExternalInput")
    peer1_h = nc.dram_tensor("peer1", [P, 8], F32, kind="ExternalInput")
    ccqi_h = nc.dram_tensor("ccqi", [P, 80], F32)
    ccqo_h = nc.dram_tensor("ccqo", [P, 80], F32)
    ccki_h = nc.dram_tensor("ccki", [P, 80], F32)
    ccko_h = nc.dram_tensor("ccko", [P, 80], F32)
    bthl_hbm = nc.dram_tensor("bthl_hbm", [2, D], BF16)
    bbhl_hbm = nc.dram_tensor("bbhl_hbm", [2, D], BF16)

    es = ExitStack()
    sb = lambda nm, shp, dt: es.enter_context(nc.sbuf_tensor(nm, shp, dt))
    psm = lambda nm, shp: es.enter_context(nc.psum_tensor(nm, shp, F32))

    B1 = sb("B1", [P, 16384], BF16)
    B2 = sb("B2", [P, 16384], BF16)
    B3 = sb("B3", [P, 16384], BF16)
    B4 = sb("B4", [P, 16384], BF16)
    H1B = sb("H1B", [P, 4096], BF16)
    WS0 = sb("WS0", [P, 8192], BF16)
    WS1 = sb("WS1", [P, 8192], BF16)
    WS1F = WS1.bitcast(F32)
    XBB = sb("XBB", [P, 2 * D], F32)
    OSB = sb("OSB", [P, 2 * D], F32)
    OS = [OSB[:, 0:D], OSB[:, D:2 * D]]
    U1 = sb("U1", [P, 512], F32)
    T2 = [sb(f"T2{i}", [P, 512], F32) for i in range(2)]
    SCRL = sb("SCRL", [P, 512], BF16)
    EB = sb("EB", [H, T], BF16)
    BNS = sb("BNS", [P, 48], F32)
    MV = sb("MV", [P, 2 * TCH], F32)
    NMRS4 = sb("NMRS4", [P, 4], F32)
    RSTD = sb("RSTD", [P, TCH], F32)
    M2S = sb("M2S", [P, 1], F32)
    NMRS = sb("NMRS", [P, 1], F32)
    WQAB = sb("WQAB", [P, P], BF16)
    WKAF = sb("WKAF", [P, P], F32)
    WKAB = sb("WKAB", [P, P], BF16)
    BQ = sb("BQ", [P, 8], F32)
    BK = sb("BK", [P, 8], F32)
    BQA = sb("BQA", [H, 1], F32)
    BKA = sb("BKA", [H, 1], F32)
    BQAS = sb("BQAS", [H, 1], F32)
    BKAS = sb("BKAS", [H, 1], F32)
    BTHL = sb("BTHL", [2, D], BF16)
    BBHL = sb("BBHL", [2, D], BF16)
    SELP = sb("SELP", [H, 1024], BF16)
    A16 = sb("A16", [H, P], BF16)
    MASKB = sb("MASKB", [H, 8], F32)
    EYE = sb("EYE", [P, P], BF16)
    ONES2 = sb("ONES2", [2, P], BF16)
    SLOT1 = sb("SLOT1", [P, 8], F32)
    PEER1 = sb("PEER1", [P, 8], F32)
    XCHQ = sb("XCHQ", [P, 10], F32)
    XCHK = sb("XCHK", [P, 10], F32)
    BIG = sb("BIG", [P, 80], F32)
    G = sb("G", [P, 80], F32)
    PEERQ = sb("PEERQ", [P, 10], F32)
    PEERK = sb("PEERK", [P, 10], F32)
    NEGM = sb("NEGM", [H, 1], F32)
    MRG = sb("MRG", [H, 16], F32)
    BAB = sb("BAB", [H, 16], BF16)
    PM1 = sb("PM1", [P, 8], F32)
    PM2 = sb("PM2", [P, 8], F32)
    PQ = sb("PQ", [P, 8], F32)
    PMK = sb("PMK", [P, 8], F32)
    PK = sb("PK", [P, 8], F32)
    ACC4 = sb("ACC4", [P, 4], F32)
    CSTS = sb("CSTS", [P, 4], F32)
    TP4 = sb("TP4", [P, 40], F32)

    psA = [psm(f"psA{i}", [P, 512]) for i in range(4)]
    psB = psm("psB", [P, T])

    sems = {}
    for nm in ("cst", "bqk", "wq", "wk", "fw", "a1", "a2", "wt", "bto", "bd", "b32",
               "bnq", "bnk", "gq", "gk", "x0", "x1", "x2", "x3", "x2e", "x2o", "oe", "oo",
               "tr0", "tr1", "tr2", "tr3", "sv", "ss",
               "mmA", "evA", "mmB", "evB", "sccq", "scck", "xf0", "xf1"):
        sems[nm] = es.enter_context(nc.semaphore(nm))

    cps = {}

    def walk(active, eng, dry):
        cnt = {k: 0 for k in sems}

        def bump(sem, k):
            cnt[sem] += k
            return cnt[sem]

        def CP(name):
            if dry:
                cps[name] = dict(cnt)

        def W(tag, sem, val):
            if not dry and active == tag and val > 0:
                eng.wait_ge(sems[sem], val)

        def WCP(tag, name, sem):
            if not dry:
                W(tag, sem, cps[name][sem])

        def VW():
            W("v", "sv", cnt["sv"])

        def E(tag, f, inc=None, inck=1):
            if not dry and active == tag:
                ins = f(eng)
                if inc is not None:
                    ins.then_inc(sems[inc], inck)
            if inc is not None:
                bump(inc, inck)

        SCR3 = BIG
        XBv = lambda i, a, b: XBB[:, (i % 2) * D + a: (i % 2) * D + b]
        QSv = XBB[0:16, :]
        XN = lambda i, a, b: B1[:, i * 1024 + a: i * 1024 + b]
        H2V = lambda c, a, b: B1[:, c * 2048 + a: c * 2048 + b]
        XNT = lambda c, a, b: B2[:, c * 2048 + a: c * 2048 + b]
        WTV = lambda k, a, b: B2[:, k * 1024 + a: k * 1024 + b]
        MQ = lambda c, a, b: B3[:, c * 2048 + a: c * 2048 + b]
        MK = lambda c, a, b: B4[:, c * 2048 + a: c * 2048 + b]
        WV = lambda c, a, b: B4[:, c * 2048 + a: c * 2048 + b]
        H1V = lambda k, a, b: H1B[:, k * 512 + a: k * 512 + b]
        WSL = lambda ws, k, a, b: ws[:, k * 1024 + a: k * 1024 + b]

        def x_pair_load(p):
            if p >= 2:
                WCP("g", f"xn{2 * (p - 2) + 1}", "sv")
            dst = bass.AP(H1B, (p % 2) * 2 * D, [[4 * D, P], [D, 2], [1, D]])
            src_ = bass.AP(x_h, 256 * p * D, [[D, P], [P * D, 2], [1, D]])
            E("g", lambda g, dst=dst, src_=src_: g.dma_start(dst, src_),
              f"x{p % 2}", 16)

        def xf_load(p):
            stg = XBB if p % 2 == 0 else OSB
            if p >= 2:
                WCP("s", f"xcast{p - 2}", "sv")
            dst = bass.AP(stg, 0, [[2 * D, P], [D, 2], [1, D]])
            src_ = bass.AP(x_h, 256 * p * D, [[D, P], [P * D, 2], [1, D]])
            E("s", lambda s, dst=dst, src_=src_: s.dma_start(dst, src_),
              f"xf{p % 2}", 16)

        def x_cast(p):
            stg = XBB if p % 2 == 0 else OSB
            W("v", f"xf{p % 2}", 16 * (p // 2 + 1))
            if p >= 2:
                WCP("v", f"xn{2 * (p - 2) + 1}", "sv")
            E("v", lambda v, p=p, stg=stg: v.tensor_copy(
                H1B[:, (p % 2) * 2 * D:(p % 2) * 2 * D + 2 * D], stg[:, :]),
              "sv")
            CP(f"xcast{p}")


        E("g", lambda g: g.dma_start(BQ[:, :], bq_h[:, :]), "bqk", 16)
        E("g", lambda g: g.dma_start(BK[:, :], bk_h[:, :]), "bqk", 16)

        def wload(name, sem, dst_handle, dst_row_elems, gate=None):
            if gate is not None:
                W("g", gate[0], gate[1])
            src = bass.AP(w_hs[name], 0, [[1024, 128], [131072, 8], [1, 1024]])
            dst = bass.AP(dst_handle, 0, [[dst_row_elems, 128], [1024, 8], [1, 1024]])
            E("g", lambda g: g.dma_start(dst, src), sem, 16)

        wload("Wq", "wq", WS0, 8192)
        for (dst, src) in ((SELP[:, :], selp_h[:, :]), (A16[:, :], a16_h[:, :]),
                           (MASKB[:, :], maskb_h[:, :]), (EYE[:, :], eye_h[:, :]),
                           (ONES2[:, :], ones2_h[:, :]), (SLOT1[:, :], slot1_h[:, :]),
                           (PEER1[:, :], peer1_h[:, :]), (BQA[:, :], bqa_h[:, :]),
                           (BKA[:, :], bka_h[:, :]), (WQAB[:, :], wqa_h[:, :]),
                           (WKAF[:, :], wka_h[:, :])):
            E("g", lambda g, d=dst, s=src: g.dma_start(d, s), "cst", 16)
        CP("consts")
        x_pair_load(4)
        x_pair_load(5)
        x_pair_load(6)
        x_pair_load(7)
        wload("Wk", "wk", WS1, 8192, gate=("evA", 4))
        def g_exchange(cci, cco, ccsem, bnsem, gsem, cpname):
            WCP("g", cpname + "big", "sv")
            E("g", lambda g, cci=cci: g.dma_start(cci[:, :], BIG[:, :]), bnsem, 16)
            W("g", bnsem, 16)
            E("g", lambda g, cci=cci, cco=cco: g.collective_compute(
                "AllReduce", ALU.add,
                replica_groups=[[0, 1], [2, 3], [4, 5], [6, 7]],
                ins=[cci.ap().opt()], outs=[cco.ap().opt()]), ccsem, 1)
            W("g", ccsem, cnt[ccsem])
            E("g", lambda g, cco=cco, gsem=gsem: g.dma_start(G[:, :], cco[:, :]),
              gsem, 16)
            CP(cpname + "gath")

        WCP("g", "bt_pair", "sv")
        E("g", lambda g: g.dma_start(
            bass.AP(bthl_hbm, 0, [[1024, 2], [1, 1024]]),
            bass.AP(EB, 0, [[2048, 1], [1, 2048]])), "bto", 16)
        WCP("g", "prep", "sv")
        E("g", lambda g: g.dma_start(
            bass.AP(bbhl_hbm, 0, [[1024, 2], [1, 1024]]),
            bass.AP(EB, 0, [[2048, 1], [1, 2048]])), "bto", 16)
        W("g", "bto", 32)
        E("g", lambda g: g.dma_start(BTHL[0:2, :], bthl_hbm[:, :]), "bd", 16)
        E("g", lambda g: g.dma_start(BBHL[0:2, :], bbhl_hbm[:, :]), "bd", 16)
        CP("biasdma")
        wload("FW", "fw", WS0, 8192, gate=("evA", 32))
        g_exchange(ccqi_h, ccqo_h, "sccq", "bnq", "gq", "q_")
        wload("A1", "a1", WS1, 8192, gate=("mmA", 64))
        g_exchange(ccki_h, ccko_h, "scck", "bnk", "gk", "k_")
        wload("A2", "a2", WS0, 8192, gate=("mmA", 120))
        W("g", "mmA", 120)
        srcWT = bass.AP(w_hs["Wt"], 0, [[1024, 128], [131072, 8], [1, 1024]])
        dstWT = bass.AP(B2, 0, [[16384, 128], [1024, 8], [1, 1024]])
        E("g", lambda g: g.dma_start(dstWT, srcWT), "wt", 16)

        def x_transpose(i):
            WCP("sy", f"xn{i}", "sv")
            out_ap = bass.AP(B2, 128 * i, [[16384, P], [2048, 8], [1, P]])
            E("sy", lambda s, i=i, out_ap=out_ap: s.dma_start_transpose(
                out_ap, XN(i, 0, D)), f"tr{i // 4}", 16)

        for i in range(TCH):
            x_transpose(i)

        for k in range(TCH // 2):
            WCP("sy", f"fin_{2 * k + 1}", "sv")
            E("sy", lambda s, k=k: s.dma_start(
                out_h[128 * (2 * k + 1):128 * (2 * k + 2), :], OS[1]),
              "oo", 16)

        E("v", lambda v: v.memset(CSTS[:, 0:1], -0.5), "sv")
        E("v", lambda v: v.memset(CSTS[:, 1:2], EPS), "sv")
        E("v", lambda v: v.memset(CSTS[:, 2:3], SCALE), "sv")

        def XC(i, a, b):
            off = ((i // 2) % 2) * 2 * D + (i % 2) * D
            return H1B[:, off + a:off + b]

        def ln_stats_chunk(i):
            if i < 8:
                VW()
            else:
                W("v", f"x{(i // 2) % 2}", 16 * (1 if i < 12 else 2))
            E("v", lambda v, i=i: v.bn_stats(
                BNS[:, 12 * (i % 4):12 * (i % 4) + 6], XC(i, 0, 512)), "sv")
            E("v", lambda v, i=i: v.bn_stats(
                BNS[:, 12 * (i % 4) + 6:12 * (i % 4) + 12], XC(i, 512, 1024)),
              "sv")
            CP(f"bns{i}")
            if i % 4 == 3:
                g = i // 4
                VW()
                for k in range(4):
                    j = 4 * g + k
                    E("v", lambda v, j=j, k=k: v.bn_aggr(
                        MV[:, 2 * j:2 * j + 2], BNS[:, 12 * k:12 * k + 12]),
                      "sv")
                CP(f"st{g}")

        def ln_xn_group(g):
            WCP("v", f"rstd{g}", "ss")
            E("v", lambda v, g=g: v.scalar_tensor_tensor(
                NMRS4[:, :], bass.AP(MV, 8 * g, [[2 * TCH, P], [2, 4]]), -1.0,
                RSTD[:, 4 * g:4 * g + 4], ALU.mult, ALU.mult), "sv")
            VW()
            for k in range(4):
                i = 4 * g + k
                E("v", lambda v, i=i, k=k: v.tensor_scalar(
                    XN(i, 0, D), XC(i, 0, D), RSTD[:, i:i + 1],
                    NMRS4[:, k:k + 1], ALU.mult, ALU.add), "sv")
                CP(f"xn{i}")

        for g in range(4):
            if g < 2:
                x_cast(2 * g)
                x_cast(2 * g + 1)
            for i in range(4 * g, 4 * g + 4):
                ln_stats_chunk(i)
            ln_xn_group(g)

        E("v", lambda v: v.memset(XCHQ[:, :], 0.0), "sv")
        E("v", lambda v: v.memset(XCHK[:, :], 0.0), "sv")
        WCP("v", "consts", "cst")
        E("v", lambda v: v.tensor_scalar(BQAS[:, :], BQA[:, :], SCALE, None,
                                         ALU.mult), "sv")
        E("v", lambda v: v.tensor_scalar(BKAS[:, :], BKA[:, :], SCALE, None,
                                         ALU.mult), "sv")
        WCP("v", "b32", "b32")
        E("v", lambda v: v.tensor_copy(EB[0:1, 0:D], OSB[0:1, 0:D]), "sv")
        VW()
        E("v", lambda v: v.tensor_sub(EB[0:1, D:2 * D], OSB[0:1, 0:D],
                                      EB[0:1, 0:D]), "sv")
        CP("bt_pair")
        W("v", "bto", 16)
        E("v", lambda v: v.tensor_copy(EB[0:1, 0:D], OSB[0:1, D:2 * D]), "sv")
        VW()
        E("v", lambda v: v.tensor_sub(EB[0:1, D:2 * D], OSB[0:1, D:2 * D],
                                      EB[0:1, 0:D]), "sv")
        CP("prep")

        SCX = lambda half, a, b: XBB[32:48, half * 1024 + a: half * 1024 + b]

        def softmax_part(xch, cpbase):
            WCP("v", cpbase + "s", "ss")
            E("v", lambda v: v.tensor_max(SCX(0, 0, 1024), QSv[:, 0:1024],
                                          QSv[:, 1024:2048]), "sv")
            w, half = 512, 1
            while w >= 1:
                oh, ih = half, 1 - half
                VW()
                E("v", lambda v, w=w, oh=oh, ih=ih: v.tensor_max(
                    SCX(oh, 0, w), SCX(ih, 0, w), SCX(ih, w, 2 * w)), "sv")
                w //= 2
                half = 1 - half
            mx = SCX(1 - half, 0, 1)
            VW()
            E("v", lambda v, mx=mx: v.tensor_scalar(NEGM[:, :], mx, -1.0,
                                                    None, ALU.mult), "sv")
            E("v", lambda v, mx=mx, xch=xch: v.tensor_copy(xch[0:16, 8:9], mx),
              "sv")
            CP(cpbase + "negm")
            WCP("v", cpbase + "eb", "ss")
            E("v", lambda v, xch=xch: v.scalar_tensor_tensor(
                QSv[:, :], EB[:, :], 1.0, EB[:, :], ALU.mult, ALU.max,
                accum_out=xch[0:16, 9:10]), "sv")
            W("v", "sv", cnt["sv"])
            CP(cpbase + "sum")

        def v_pool(xch, srcf, cpname):
            base_mmB = cps[cpname + "erep0"]["mmB"] if not dry else 0
            idx = 0
            for c in range(NCH):
                for n in range(NT):
                    if not dry:
                        W("v", "mmB", base_mmB + idx + 1)
                    sl = (base_mmB + idx) % 4 if not dry else 0
                    E("v", lambda v, c=c, n=n, sl=sl, srcf=srcf:
                      v.scalar_tensor_tensor(
                          SCRL[:, 0:512], srcf(c, 512 * n, 512 * n + 512), 1.0,
                          psB[:, 512 * sl:512 * sl + 512], ALU.mult, ALU.mult,
                          accum_out=ACC4[:, n:n + 1]), "evB")
                    idx += 1
                W("v", "evB", cnt["evB"])
                VW()
                E("v", lambda v: v.tensor_add(M2S[:, :], ACC4[:, 0:1],
                                              ACC4[:, 1:2]), "sv")
                E("v", lambda v: v.tensor_add(NMRS[:, :], ACC4[:, 2:3],
                                              ACC4[:, 3:4]), "sv")
                VW()
                E("v", lambda v, c=c, xch=xch: v.tensor_add(
                    xch[:, c:c + 1], M2S[:, :], NMRS[:, :]), "sv")
            CP(cpname + "pool")

        def v_big(xch, cpname):
            VW()
            E("v", lambda v, xch=xch: v.tensor_tensor(
                bass.AP(BIG, 0, [[80, P], [10, 8], [1, 10]]),
                bass.AP(xch, 0, [[10, P], [0, 8], [1, 10]]),
                bass.AP(SLOT1, 0, [[8, P], [1, 8], [0, 10]]), ALU.mult), "sv")
            CP(cpname + "big")

        def v_merge(xch, peer, cpname, dstP):
            W("v", "gq" if cpname == "q_" else "gk", 16)
            for r in range(8):
                E("v", lambda v, r=r: v.tensor_scalar(
                    SCR3[:, 10 * r:10 * r + 10], G[:, 10 * r:10 * r + 10],
                    PEER1[:, r:r + 1], None, ALU.mult), "sv")
            VW()
            for p_ in range(4):
                E("v", lambda v, p_=p_: v.tensor_add(
                    TP4[:, 10 * p_:10 * p_ + 10],
                    SCR3[:, 20 * p_:20 * p_ + 10],
                    SCR3[:, 20 * p_ + 10:20 * p_ + 20]), "sv")
            VW()
            E("v", lambda v: v.tensor_add(SCR3[:, 0:10], TP4[:, 0:10],
                                          TP4[:, 10:20]), "sv")
            E("v", lambda v: v.tensor_add(SCR3[:, 10:20], TP4[:, 20:30],
                                          TP4[:, 30:40]), "sv")
            VW()
            E("v", lambda v, peer=peer: v.tensor_add(
                peer[:, :], SCR3[:, 0:10], SCR3[:, 10:20]), "sv")
            VW()
            E("v", lambda v, xch=xch, peer=peer: v.tensor_tensor(
                MRG[:, 0:1], xch[0:16, 8:9], peer[0:16, 8:9], ALU.max), "sv")
            VW()
            E("v", lambda v: v.tensor_scalar(MRG[:, 1:2], MRG[:, 0:1], -1.0,
                                             None, ALU.mult), "sv")
            CP(cpname + "negM")
            WCP("v", cpname + "fab", "ss")
            E("v", lambda v, xch=xch: v.tensor_mul(MRG[:, 4:5], xch[0:16, 9:10],
                                                   MRG[:, 2:3]), "sv")
            E("v", lambda v, peer=peer: v.tensor_mul(MRG[:, 5:6], peer[0:16, 9:10],
                                                     MRG[:, 3:4]), "sv")
            VW()
            E("v", lambda v: v.tensor_add(MRG[:, 6:7], MRG[:, 4:5], MRG[:, 5:6]), "sv")
            VW()
            E("v", lambda v: v.reciprocal(MRG[:, 7:8], MRG[:, 6:7]), "sv")
            VW()
            E("v", lambda v: v.tensor_mul(MRG[:, 8:9], MRG[:, 2:3], MRG[:, 7:8]), "sv")
            E("v", lambda v: v.tensor_mul(MRG[:, 9:10], MRG[:, 3:4], MRG[:, 7:8]), "sv")
            VW()
            E("v", lambda v: v.tensor_tensor(
                BAB[:, 0:8], bass.AP(MRG, 8, [[16, 16], [0, 8]]), MASKB[:, :],
                ALU.mult), "sv")
            E("v", lambda v: v.tensor_tensor(
                BAB[:, 8:16], bass.AP(MRG, 9, [[16, 16], [0, 8]]), MASKB[:, :],
                ALU.mult), "sv")
            CP(cpname + "bab")
            if not dry:
                W("v", "mmB", cps[cpname + "rep"]["mmB"])
            sl = (cps[cpname + "rep"]["mmB"] - 1) % 4 if not dry else 0
            E("v", lambda v, xch=xch, sl=sl: v.tensor_mul(
                PM1[:, :], xch[:, 0:8], psB[:, 512 * sl:512 * sl + 8]), "sv")
            E("v", lambda v, peer=peer, sl=sl: v.tensor_mul(
                PM2[:, :], peer[:, 0:8], psB[:, 512 * sl + 8:512 * sl + 16]),
              "evB")
            VW()
            E("v", lambda v, dstP=dstP: v.tensor_add(dstP[:, :], PM1[:, :],
                                                     PM2[:, :]), "sv")
            CP(cpname + "P")

        softmax_part(XCHQ, "q_")
        v_pool(XCHQ, MQ, "q_")
        v_big(XCHQ, "q_")
        v_merge(XCHQ, PEERQ, "q_", PQ)
        VW()
        for c in range(NCH):
            E("v", lambda v, c=c: v.tensor_scalar(
                WKAB[:, c * 16:(c + 1) * 16], WKAF[:, c * 16:(c + 1) * 16],
                PQ[:, c:c + 1], None, ALU.mult), "sv")
        CP("wkab")
        softmax_part(XCHK, "k_")
        v_pool(XCHK, MK, "k_")
        v_big(XCHK, "k_")
        v_merge(XCHK, PEERK, "k_", PMK)
        VW()
        E("v", lambda v: v.tensor_mul(PK[:, :], PQ[:, :], PMK[:, :]), "sv")
        VW()
        for c in range(NCH):
            E("v", lambda v, c=c: v.tensor_scalar(
                WV(c, 0, 2048), MQ(c, 0, 2048), PK[:, c:c + 1], None, ALU.mult),
              "sv")
        CP("wv")


        def s_rstd_group(g):
            WCP("s", f"st{g}", "sv")
            var4 = bass.AP(MV, 8 * g + 1, [[2 * TCH, P], [2, 4]])
            E("s", lambda s, var4=var4: s.activation(var4, var4, AF.Ln,
                                                     bias=CSTS[:, 1:2]), "ss")
            E("s", lambda s, g=g, var4=var4: s.activation(
                RSTD[:, 4 * g:4 * g + 4], var4, AF.Exp, scale=CSTS[:, 0:1]),
              "ss")
            CP(f"rstd{g}")

        for p in range(4):
            xf_load(p)
        s_rstd_group(0)
        s_rstd_group(1)

        def s_late_ln(tid):
            if tid == 8:
                s_rstd_group(2)
            if tid == 16:
                s_rstd_group(3)
            if tid == 20:
                WCP("s", f"xn{TCH - 1}", "sv")
                E("s", lambda s: s.dma_start(OSB[0:1, 0:D], bt_h[:, :]),
                  "b32", 16)
                E("s", lambda s: s.dma_start(OSB[0:1, D:2 * D], bb_h[:, :]),
                  "b32", 16)
                CP("b32")

        def proj_tile(ws, m, movf, mov_a, dst_ap, bias_ap, pool=4):
            bi = cnt["mmA"] % pool
            bank = psA[bi] if bi < 4 else psB[:, 512 * (bi - 4):512 * (bi - 4) + 512]
            if cnt["mmA"] >= pool:
                W("t", "evA", cnt["mmA"] - (pool - 1))
            for k in range(NCH):
                E("t", lambda t, ws=ws, k=k, m=m, bank=bank:
                  t.matmul(bank[:, :] if not isinstance(bank, bass.AP) else bank,
                           WSL(ws, k, m * 128, (m + 1) * 128),
                           movf(k, mov_a, mov_a + 512),
                           start=(k == 0), stop=(k == NCH - 1)),
                  "mmA" if k == NCH - 1 else None)
            tid = cnt["mmA"]
            W("s", "mmA", tid)
            if tid == 1:
                W("s", "bqk", 32)
            s_late_ln(tid)
            bank_ap = bank if isinstance(bank, bass.AP) else bank[:, :]
            if bias_ap is None:
                E("s", lambda s, dst_ap=dst_ap, bank_ap=bank_ap:
                  s.activation(dst_ap, bank_ap, AF.Identity), "evA")
            else:
                E("s", lambda s, dst_ap=dst_ap, bank_ap=bank_ap, bias_ap=bias_ap:
                  s.activation(dst_ap, bank_ap, AF.Identity, bias=bias_ap),
                  "evA")

        W("t", "wq", 16)
        for wi in range(4):
            WCP("t", f"xn{wi}", "sv")
            for wj in range(10):
                E("t", lambda t: t.matmul(psB[:, 0:512], WS0[:, 0:128],
                                          WS0[:, 0:512], start=True, stop=True),
                  None)
        for wi in range(3):
            W("t", "tr0", 16 * (wi + 1))
            for wj in range(8):
                E("t", lambda t: t.matmul(psB[:, 0:512], WS0[:, 0:128],
                                          WS0[:, 0:512], start=True, stop=True),
                  None)

        for n in range(NT):
            W("t", f"tr{n}", 16 * 4)
            W("t", "wq", 16)
            for m in range(NCH):
                proj_tile(WS0, m, XNT, 512 * n, MQ(m, 512 * n, 512 * n + 512),
                          BQ[:, m:m + 1], pool=8)
        CP("step1")

        def mk_tiles(n_list, hook=None):
            W("t", "wk", 16)
            ti = 0
            for n in n_list:
                W("t", f"tr{n}", 16 * 4)
                for m in range(NCH):
                    proj_tile(WS1, m, XNT, 512 * n,
                              MK(m, 512 * n, 512 * n + 512), BK[:, m:m + 1])
                    if hook is not None:
                        hook(ti)
                    ti += 1

        def score_phase(stat_buf, movf, bias_sc, cpbase, ev_gate):
            for n in range(NT):
                if ev_gate is not None:
                    W("t", "evA", ev_gate(n))
                if cnt["mmB"] >= 4:
                    W("t", "evB", cnt["mmB"] - 3)
                for k in range(NCH):
                    E("t", lambda t, n=n, k=k, stat_buf=stat_buf:
                      t.matmul(psB[0:16, 512 * n:512 * n + 512],
                               stat_buf[:, k * 16:(k + 1) * 16],
                               movf(k, 512 * n, 512 * n + 512),
                               start=(k == 0), stop=(k == NCH - 1)),
                      "mmB" if k == NCH - 1 else None)
            W("s", "mmB", cnt["mmB"])
            WCP("s", "prep", "sv")
            WCP("s", f"xn{TCH - 1}", "sv")
            E("s", lambda s, bias_sc=bias_sc: s.activation(
                QSv, psB[0:16, :], AF.Identity, bias=bias_sc,
                scale=CSTS[0:16, 2:3]), "ss")
            E("s", lambda s: s.nop(), "evB", 4)
            CP(cpbase + "s")
            WCP("s", cpbase + "negm", "sv")
            if cpbase == "q_":
                W("s", "bd", 32)
            if cpbase == "k_":
                if not dry:
                    W("s", "mmB", cps["q_erep_done"]["mmB"])
            E("s", lambda s: s.activation(EB[:, :], QSv, AF.Exp,
                                          bias=NEGM[:, :]), "ss")
            CP(cpbase + "eb")

        def erep_group(cpname, c):
            if c == 0:
                WCP("t", cpname + "eb", "ss")
                CP(cpname + "erep0")
            for n in range(NT):
                sl = cnt["mmB"] % 4
                if cnt["mmB"] >= 4:
                    W("t", "evB", cnt["mmB"] - 3)
                E("t", lambda t, c=c, n=n, sl=sl:
                  t.matmul(psB[:, 512 * sl:512 * sl + 512],
                           SELP[:, c * 128:(c + 1) * 128],
                           EB[0:16, 512 * n:512 * n + 512],
                           start=True, stop=True), "mmB")
            if c == NCH - 1:
                CP(cpname + "erep_done")

        def rep_mms(cpname):
            WCP("t", cpname + "bab", "sv")
            sl = cnt["mmB"] % 4
            if cnt["mmB"] >= 4:
                W("t", "evB", cnt["mmB"] - 3)
            E("t", lambda t, sl=sl: t.matmul(psB[:, 512 * sl:512 * sl + 8],
                                             A16[:, :], BAB[:, 0:8],
                                             start=True, stop=True), None)
            E("t", lambda t, sl=sl: t.matmul(psB[:, 512 * sl + 8:512 * sl + 16],
                                             A16[:, :], BAB[:, 8:16],
                                             start=True, stop=True), "mmB")
            CP(cpname + "rep")

        def s_fab(cpname, xch, peer):
            WCP("s", cpname + "negM", "sv")
            E("s", lambda s, xch=xch: s.activation(MRG[:, 2:3], xch[0:16, 8:9],
                                                   AF.Exp, bias=MRG[0:16, 1:2]),
              "ss")
            E("s", lambda s, peer=peer: s.activation(MRG[:, 3:4], peer[0:16, 8:9],
                                                     AF.Exp, bias=MRG[0:16, 1:2]),
              "ss")
            CP(cpname + "fab")

        def h1_phase(n):
            W("t", "fw", 16)
            for m in range(NCH):
                proj_tile(WS0, m, XNT, 512 * n, H1V(m, 0, 512), None)

        def h2_phase(n, first, hook=None):
            W("t", "a1", 16)
            W("t", "evA", cnt["mmA"])
            for m in range(NCH):
                bank = psA[cnt["mmA"] % 4]
                if cnt["mmA"] >= 4:
                    W("t", "evA", cnt["mmA"] - 3)
                for k in range(NCH):
                    E("t", lambda t, k=k, m=m, bank=bank:
                      t.matmul(bank[:, :], WSL(WS1, k, m * 128, (m + 1) * 128),
                               H1V(k, 0, 512),
                               start=(k == 0), stop=(k == NCH - 1)),
                      "mmA" if k == NCH - 1 else None)
                tid = cnt["mmA"]
                W("s", "mmA", tid)
                if first and m == 0:
                    for bn in range(4):
                        W("s", f"tr{bn}", 16 * 4)
                E("s", lambda s, m=m, n=n, bank=bank: s.activation(
                    H2V(m, 512 * n, 512 * n + 512), bank[:, :], AF.Identity),
                  "evA")
                if hook is not None:
                    hook(m)

        W("t", "evA", 32)
        WCP("t", "consts", "cst")
        score_phase(WQAB, MQ, BQAS[:, :], "q_", None)
        mk_tiles([0])
        mk_tiles([1], hook=lambda ti: erep_group("q_", ti) if ti < 8 else None)
        mk_tiles([2, 3])
        h1_phase(0)
        s_fab("q_", XCHQ, PEERQ)
        rep_mms("q_")
        W("t", "evA", 64)
        WCP("t", "wkab", "sv")
        score_phase(WKAB, MK, BKAS[:, :], "k_", None)
        h2_phase(0, True)
        h1_phase(1)
        h2_phase(1, False,
                 hook=lambda m: erep_group("k_", m) if m < 8 else None)
        h1_phase(2)
        h2_phase(2, False)
        s_fab("k_", XCHK, PEERK)
        rep_mms("k_")
        h1_phase(3)
        h2_phase(3, False)
        CP("h2done")

        W("t", "bd", 32)
        bld = []
        for n in range(4):
            src_hl, j = (BTHL, BBHL)[n // 2], n % 2
            bank = psA[cnt["mmA"] % 4]
            if cnt["mmA"] >= 4:
                W("t", "evA", cnt["mmA"] - 3)
            E("t", lambda t, src_hl=src_hl, j=j, bank=bank:
              t.matmul(bank[:, :], ONES2[:, :], src_hl[:, 512 * j:512 * j + 512],
                       start=True, stop=True), "mmA")
            bld.append(cnt["mmA"])
        for n in range(4):
            bank = psA[(bld[n] - 1) % 4]
            W("v", "mmA", bld[n])
            E("v", lambda v, n=n, bank=bank: v.tensor_copy(
                WS1F[:, 512 * n:512 * n + 512], bank[:, :]), "sv")
            E("v", lambda v: v.nop(), "evA", 1)

        def x2_load(i):
            if i >= 2:
                WCP("g", f"fin_u1_{i - 2}", "sv")
            else:
                WCP("g", "k_eb", "ss")
                WCP("g", "k_sum", "sv")
            E("g", lambda g, i=i: g.dma_start(XBv(i, 0, D),
                                              x_h[128 * i:128 * (i + 1), :]),
              "x2e" if i % 2 == 0 else "x2o", 16)

        x2_load(0)
        x2_load(1)

        W("t", "a2", 16)
        W("t", "wt", 16)
        W("t", "bd", 32)
        WCP("t", "wv", "sv")
        for i in range(TCH):
            tids = []
            for j in range(2):
                bank = psA[cnt["mmA"] % 4]
                if cnt["mmA"] >= 4:
                    W("t", "evA", cnt["mmA"] - 3)
                for k in range(NCH):
                    E("t", lambda t, i=i, j=j, k=k, bank=bank:
                      t.matmul(bank[:, :], WV(k, 128 * i, 128 * i + 128),
                               WTV(k, 512 * j, 512 * j + 512),
                               start=(k == 0), stop=False), None)
                for q in range(4):
                    c = 4 * j + q
                    E("t", lambda t, i=i, c=c, q=q, bank=bank:
                      t.matmul(bank[:, 128 * q:128 * q + 128],
                               MQ(c, 128 * i, 128 * i + 128), EYE[:, :],
                               start=False, stop=(q == 3)),
                      "mmA" if q == 3 else None)
                tids.append(cnt["mmA"])
            for j in range(2):
                bank = psA[cnt["mmA"] % 4]
                if cnt["mmA"] >= 4:
                    W("t", "evA", cnt["mmA"] - 3)
                for k in range(NCH):
                    E("t", lambda t, i=i, j=j, k=k, bank=bank:
                      t.matmul(bank[:, :], H2V(k, 128 * i, 128 * i + 128),
                               WSL(WS0, k, 512 * j, 512 * j + 512),
                               start=(k == 0), stop=(k == NCH - 1)),
                      "mmA" if k == NCH - 1 else None)
                tids.append(cnt["mmA"])

            FBV = [WS1F[:, 2048 + 512 * jj:2560 + 512 * jj] for jj in range(2)]
            U1B = [WS1F[:, 3072 + 512 * jj:3584 + 512 * jj] for jj in range(2)]
            for j in range(2):
                W("v", "mmA", tids[j])
                if j == 0:
                    W("v", "x2e" if i % 2 == 0 else "x2o", 16 * (i // 2 + 1))
                    if i >= 2:
                        W("v", "oe" if i % 2 == 0 else "oo",
                          16 * ((i - 2) // 2 + 1))
                    if i == 0:
                        VW()
                E("v", lambda v, i=i, j=j, a_bank=psA[(tids[j] - 1) % 4]:
                  v.tensor_add(U1B[j], a_bank[:, :],
                               XBv(i, 512 * j, 512 * j + 512)), "sv")
                E("v", lambda v: v.nop(), "evA", 1)
            CP(f"fin_u1_{i}")
            for j in range(2):
                W("v", "mmA", tids[2 + j])
                E("v", lambda v, j=j, f_bank=psA[(tids[2 + j] - 1) % 4]:
                  v.tensor_add(FBV[j], f_bank[:, :],
                               WS1F[:, 1024 + 512 * j:1536 + 512 * j]), "sv")
                E("v", lambda v: v.nop(), "evA", 1)
                CP(f"fin_fb_{i}_{j}")

            for j in range(2):
                WCP("s", f"fin_fb_{i}_{j}", "sv")
                if i >= 1:
                    WCP("s", f"fin_og_{i - 1}_{j}", "sv")
                E("s", lambda s, j=j: s.activation(
                    T2[j][:, :], FBV[j][:, :], AF.Tanh), "ss")
                CP(f"fin_t2_{i}_{j}")

            VW()
            for j in range(2):
                E("v", lambda v, j=j: v.tensor_add(
                    U1B[j], U1B[j], WS1F[:, 512 * j:512 * j + 512]), "sv")
            for j in range(2):
                WCP("v", f"fin_t2_{i}_{j}", "ss")
                VW()
                E("v", lambda v, i=i, j=j: v.tensor_add(
                    OSB[:, (i % 2) * D + 512 * j:(i % 2) * D + 512 * j + 512],
                    U1B[j], T2[j][:, :]), "sv")
                CP(f"fin_og_{i}_{j}")
            CP(f"fin_{i}")

            if i % 2 == 0:
                WCP("s", f"fin_{i}", "sv")
                E("s", lambda s, i=i: s.dma_start(
                    out_h[128 * i:128 * (i + 1), :], OS[i % 2]), "oe", 16)
            if i + 2 < TCH:
                x2_load(i + 2)

        W("s", "oe", 16 * (TCH // 2))
        W("s", "oo", 16 * (TCH // 2))
        return cnt

    final_counts = walk("dry", None, True)
    for nm, snap in cps.items():
        for k, v in snap.items():
            assert v <= final_counts[k], (nm, k, v, final_counts[k])

    with nc.Block() as block:
        @block.gpsimd
        def _(g):
            walk("g", g, False)

        @block.sync
        def _(sy):
            walk("sy", sy, False)

        @block.vector
        def _(v):
            walk("v", v, False)

        @block.scalar
        def _(s):
            walk("s", s, False)

        @block.tensor
        def _(t):
            walk("t", t, False)

    es.close()
    return nc


def make_in_maps(g):
    x = np.asarray(g["L0_attn_hidden_states"], dtype=np.float32)
    Wq = np.asarray(g["L0_attn_Wq"], np.float32)
    Wk = np.asarray(g["L0_attn_Wk"], np.float32)
    Wt = np.asarray(g["L0_attn_Wt"], np.float32)
    FW = np.asarray(g["L0_ffn_weight"], np.float32)
    A1 = np.asarray(g["L0_ffn_aux_weight1"], np.float32)
    A2 = np.asarray(g["L0_ffn_aux_weight2"], np.float32)
    Wqa = np.asarray(g["L0_attn_Wqa"], np.float32)
    Wka = np.asarray(g["L0_attn_Wka"], np.float32)
    bq = np.asarray(g["L0_attn_bq"], np.float32)
    bk = np.asarray(g["L0_attn_bk"], np.float32)
    bqa = np.asarray(g["L0_attn_bqa"], np.float32)
    bka = np.asarray(g["L0_attn_bka"], np.float32)
    bt = np.asarray(g["L0_attn_bt"], np.float32)
    bb = np.asarray(g["L0_ffn_bias"], np.float32)

    C = np.ascontiguousarray
    wqa_l = C(Wqa.reshape(8, 128, 16).transpose(1, 0, 2).reshape(128, 128))
    wka_l = C(Wka.reshape(8, 128, 16).transpose(1, 0, 2).reshape(128, 128))
    bq_l = C(bq.reshape(8, 128).T)
    bk_l = C(bk.reshape(8, 128).T)
    bqa_l = C(bqa.reshape(16, 1))
    bka_l = C(bka.reshape(16, 1))
    bt_l = C(bt.reshape(1, 1024))
    bb_l = C(bb.reshape(1, 1024))

    selp = np.zeros((16, 8, 128), np.float32)
    for c in range(8):
        selp[2 * c, c, 0:64] = 1.0
        selp[2 * c + 1, c, 64:128] = 1.0
    selp = C(selp.reshape(16, 1024).astype(BF))
    a16 = np.zeros((16, 128), np.float32)
    a16[0::2, 0:64] = 1.0
    a16[1::2, 64:128] = 1.0
    a16 = C(a16.astype(BF))
    maskb = np.zeros((16, 8), np.float32)
    for h in range(16):
        maskb[h, h // 2] = 1.0
    eye = C(np.eye(128, dtype=np.float32).astype(BF))
    ones2 = C(np.ones((2, 128), np.float32).astype(BF))

    in_maps = []
    for c in range(8):
        b, hf = c // 2, c % 2
        slab = C(x[b, hf * 2048:(hf + 1) * 2048, :])
        slot1 = np.zeros((128, 8), np.float32); slot1[:, c] = 1.0
        peer1 = np.zeros((128, 8), np.float32); peer1[:, c ^ 1] = 1.0
        in_maps.append({
            "x": slab, "Wq": Wq, "Wk": Wk, "Wt": Wt, "FW": FW, "A1": A1,
            "A2": A2, "WqaL": wqa_l, "WkaL": wka_l, "bqL": bq_l, "bkL": bk_l,
            "bqaL": bqa_l, "bkaL": bka_l, "btL": bt_l, "bbL": bb_l,
            "selp": selp, "a16": a16, "maskb": maskb, "eye": eye,
            "ones2": ones2, "slot1": slot1, "peer1": peer1,
        })
    return in_maps


def kernel(**inputs):
    g = {k: np.asarray(v) for k, v in inputs.items()}
    in_maps = make_in_maps(g)
    nc = build_nc()
    res = run_bass_kernel_spmd(nc, in_maps, core_ids=list(range(8)))
    out = np.empty((4, 4096, 1024), np.float32)
    for c in range(8):
        b, hf = c // 2, c % 2
        out[b, hf * 2048:(hf + 1) * 2048, :] = res.results[c]["out"]
    return out


# version 34
# speedup vs baseline: 1.1598x; 1.0341x over previous
import numpy as np
import ml_dtypes
import concourse.bass as bass
import concourse.mybir as mybir
from concourse.bass_utils import run_bass_kernel_spmd
from contextlib import ExitStack

F32 = mybir.dt.float32
BF16 = mybir.dt.bfloat16
AF = mybir.ActivationFunctionType
ALU = mybir.AluOpType
AX = mybir.AxisListType

BF = ml_dtypes.bfloat16
P = 128
T = 2048
D = 1024
H = 16
NCH = 8
TCH = 16
NT = 4
SCALE = 0.125
EPS = 1e-5


def build_nc():
    nc = bass.Bass("TRN2", target_bir_lowering=False)

    x_h = nc.dram_tensor("x", [T, D], F32, kind="ExternalInput")
    out_h = nc.dram_tensor("out", [T, D], F32, kind="ExternalOutput")
    w_hs = {}
    for nm in ("Wq", "Wk", "Wt", "FW", "A1", "A2"):
        w_hs[nm] = nc.dram_tensor(nm, [D, D], F32, kind="ExternalInput")
    wqa_h = nc.dram_tensor("WqaL", [P, P], F32, kind="ExternalInput")
    wka_h = nc.dram_tensor("WkaL", [P, P], F32, kind="ExternalInput")
    bq_h = nc.dram_tensor("bqL", [P, 8], F32, kind="ExternalInput")
    bk_h = nc.dram_tensor("bkL", [P, 8], F32, kind="ExternalInput")
    bqa_h = nc.dram_tensor("bqaL", [H, 1], F32, kind="ExternalInput")
    bka_h = nc.dram_tensor("bkaL", [H, 1], F32, kind="ExternalInput")
    bt_h = nc.dram_tensor("btL", [1, D], F32, kind="ExternalInput")
    bb_h = nc.dram_tensor("bbL", [1, D], F32, kind="ExternalInput")
    selp_h = nc.dram_tensor("selp", [H, 1024], BF16, kind="ExternalInput")
    a16_h = nc.dram_tensor("a16", [H, P], BF16, kind="ExternalInput")
    maskb_h = nc.dram_tensor("maskb", [H, 8], F32, kind="ExternalInput")
    eye_h = nc.dram_tensor("eye", [P, P], BF16, kind="ExternalInput")
    ones2_h = nc.dram_tensor("ones2", [2, P], BF16, kind="ExternalInput")
    slot1_h = nc.dram_tensor("slot1", [P, 8], F32, kind="ExternalInput")
    peer1_h = nc.dram_tensor("peer1", [P, 8], F32, kind="ExternalInput")
    ccqi_h = nc.dram_tensor("ccqi", [P, 80], F32)
    ccqo_h = nc.dram_tensor("ccqo", [P, 80], F32)
    ccki_h = nc.dram_tensor("ccki", [P, 80], F32)
    ccko_h = nc.dram_tensor("ccko", [P, 80], F32)
    bthl_hbm = nc.dram_tensor("bthl_hbm", [2, D], BF16)
    bbhl_hbm = nc.dram_tensor("bbhl_hbm", [2, D], BF16)

    es = ExitStack()
    sb = lambda nm, shp, dt: es.enter_context(nc.sbuf_tensor(nm, shp, dt))
    psm = lambda nm, shp: es.enter_context(nc.psum_tensor(nm, shp, F32))

    B1 = sb("B1", [P, 16384], BF16)
    B2 = sb("B2", [P, 16384], BF16)
    B3 = sb("B3", [P, 16384], BF16)
    B4 = sb("B4", [P, 16384], BF16)
    H1B = sb("H1B", [P, 4096], BF16)
    WS0 = sb("WS0", [P, 8192], BF16)
    WS1 = sb("WS1", [P, 8192], BF16)
    WS1F = WS1.bitcast(F32)
    XBB = sb("XBB", [P, 2 * D], F32)
    OSB = sb("OSB", [P, 2 * D], F32)
    OS = [OSB[:, 0:D], OSB[:, D:2 * D]]
    U1 = sb("U1", [P, 512], F32)
    T2 = [sb(f"T2{i}", [P, 512], F32) for i in range(2)]
    SCRL = sb("SCRL", [P, 512], BF16)
    EB = sb("EB", [H, T], BF16)
    BNS = sb("BNS", [P, 48], F32)
    MV = sb("MV", [P, 2 * TCH], F32)
    NMRS4 = sb("NMRS4", [P, 4], F32)
    RSTD = sb("RSTD", [P, TCH], F32)
    M2S = sb("M2S", [P, 1], F32)
    NMRS = sb("NMRS", [P, 1], F32)
    WQAB = sb("WQAB", [P, P], BF16)
    WKAF = sb("WKAF", [P, P], F32)
    WKAB = sb("WKAB", [P, P], BF16)
    BQ = sb("BQ", [P, 8], F32)
    BK = sb("BK", [P, 8], F32)
    BQA = sb("BQA", [H, 1], F32)
    BKA = sb("BKA", [H, 1], F32)
    BQAS = sb("BQAS", [H, 1], F32)
    BKAS = sb("BKAS", [H, 1], F32)
    BTHL = sb("BTHL", [2, D], BF16)
    BBHL = sb("BBHL", [2, D], BF16)
    SELP = sb("SELP", [H, 1024], BF16)
    A16 = sb("A16", [H, P], BF16)
    MASKB = sb("MASKB", [H, 8], F32)
    EYE = sb("EYE", [P, P], BF16)
    ONES2 = sb("ONES2", [2, P], BF16)
    SLOT1 = sb("SLOT1", [P, 8], F32)
    PEER1 = sb("PEER1", [P, 8], F32)
    XCHQ = sb("XCHQ", [P, 10], F32)
    XCHK = sb("XCHK", [P, 10], F32)
    BIG = sb("BIG", [P, 80], F32)
    G = sb("G", [P, 80], F32)
    PEERQ = sb("PEERQ", [P, 10], F32)
    PEERK = sb("PEERK", [P, 10], F32)
    NEGM = sb("NEGM", [H, 1], F32)
    MRG = sb("MRG", [H, 16], F32)
    BAB = sb("BAB", [H, 16], BF16)
    PM1 = sb("PM1", [P, 8], F32)
    PM2 = sb("PM2", [P, 8], F32)
    PQ = sb("PQ", [P, 8], F32)
    PMK = sb("PMK", [P, 8], F32)
    PK = sb("PK", [P, 8], F32)
    ACC4 = sb("ACC4", [P, 4], F32)
    CSTS = sb("CSTS", [P, 4], F32)
    TP4 = sb("TP4", [P, 40], F32)

    psA = [psm(f"psA{i}", [P, 512]) for i in range(4)]
    psB = psm("psB", [P, T])

    sems = {}
    for nm in ("cst", "bqk", "wq", "wk", "fw", "a1", "a2", "wt", "bto", "bd", "b32",
               "bnq", "bnk", "gq", "gk", "x0", "x1", "x2", "x3", "x2e", "x2o", "oe", "oo",
               "tr0", "tr1", "tr2", "tr3", "sv", "ss",
               "mmA", "evA", "mmB", "evB", "sccq", "scck", "xf0", "xf1"):
        sems[nm] = es.enter_context(nc.semaphore(nm))

    cps = {}

    def walk(active, eng, dry):
        cnt = {k: 0 for k in sems}

        def bump(sem, k):
            cnt[sem] += k
            return cnt[sem]

        def CP(name):
            if dry:
                cps[name] = dict(cnt)

        def W(tag, sem, val):
            if not dry and active == tag and val > 0:
                eng.wait_ge(sems[sem], val)

        def WCP(tag, name, sem):
            if not dry:
                W(tag, sem, cps[name][sem])

        def VW():
            W("v", "sv", cnt["sv"])

        def E(tag, f, inc=None, inck=1):
            if not dry and active == tag:
                ins = f(eng)
                if inc is not None:
                    ins.then_inc(sems[inc], inck)
            if inc is not None:
                bump(inc, inck)

        SCR3 = BIG
        XBv = lambda i, a, b: XBB[:, (i % 2) * D + a: (i % 2) * D + b]
        QSv = XBB[0:16, :]
        XN = lambda i, a, b: B1[:, i * 1024 + a: i * 1024 + b]
        H2V = lambda c, a, b: B1[:, c * 2048 + a: c * 2048 + b]
        XNT = lambda c, a, b: B2[:, c * 2048 + a: c * 2048 + b]
        WTV = lambda k, a, b: B2[:, k * 1024 + a: k * 1024 + b]
        MQ = lambda c, a, b: B3[:, c * 2048 + a: c * 2048 + b]
        MK = lambda c, a, b: B4[:, c * 2048 + a: c * 2048 + b]
        WV = lambda c, a, b: B4[:, c * 2048 + a: c * 2048 + b]
        H1V = lambda k, a, b: H1B[:, k * 512 + a: k * 512 + b]
        WSL = lambda ws, k, a, b: ws[:, k * 1024 + a: k * 1024 + b]

        def x_pair_load(p):
            if p >= 2:
                WCP("g", f"xn{2 * (p - 2) + 1}", "sv")
            dst = bass.AP(H1B, (p % 2) * 2 * D, [[4 * D, P], [D, 2], [1, D]])
            src_ = bass.AP(x_h, 256 * p * D, [[D, P], [P * D, 2], [1, D]])
            E("g", lambda g, dst=dst, src_=src_: g.dma_start(dst, src_),
              f"x{p % 2}", 16)

        def xf_load(p):
            stg = XBB if p % 2 == 0 else OSB
            if p >= 2:
                WCP("s", f"xcast{p - 2}", "sv")
            dst = bass.AP(stg, 0, [[2 * D, P], [D, 2], [1, D]])
            src_ = bass.AP(x_h, 256 * p * D, [[D, P], [P * D, 2], [1, D]])
            E("s", lambda s, dst=dst, src_=src_: s.dma_start(dst, src_),
              f"xf{p % 2}", 16)

        def x_cast(p):
            stg = XBB if p % 2 == 0 else OSB
            W("v", f"xf{p % 2}", 16 * (p // 2 + 1))
            if p >= 2:
                WCP("v", f"xn{2 * (p - 2) + 1}", "sv")
            E("v", lambda v, p=p, stg=stg: v.tensor_copy(
                H1B[:, (p % 2) * 2 * D:(p % 2) * 2 * D + 2 * D], stg[:, :]),
              "sv")
            CP(f"xcast{p}")


        E("g", lambda g: g.dma_start(BQ[:, :], bq_h[:, :]), "bqk", 16)
        E("g", lambda g: g.dma_start(BK[:, :], bk_h[:, :]), "bqk", 16)

        def wload(name, sem, dst_handle, dst_row_elems, gate=None):
            if gate is not None:
                W("g", gate[0], gate[1])
            src = bass.AP(w_hs[name], 0, [[1024, 128], [131072, 8], [1, 1024]])
            dst = bass.AP(dst_handle, 0, [[dst_row_elems, 128], [1024, 8], [1, 1024]])
            E("g", lambda g: g.dma_start(dst, src), sem, 16)

        wload("Wq", "wq", WS0, 8192)
        for (dst, src) in ((SELP[:, :], selp_h[:, :]), (A16[:, :], a16_h[:, :]),
                           (MASKB[:, :], maskb_h[:, :]), (EYE[:, :], eye_h[:, :]),
                           (ONES2[:, :], ones2_h[:, :]), (SLOT1[:, :], slot1_h[:, :]),
                           (PEER1[:, :], peer1_h[:, :]), (BQA[:, :], bqa_h[:, :]),
                           (BKA[:, :], bka_h[:, :]), (WQAB[:, :], wqa_h[:, :]),
                           (WKAF[:, :], wka_h[:, :])):
            E("g", lambda g, d=dst, s=src: g.dma_start(d, s), "cst", 16)
        CP("consts")
        wload("Wk", "wk", WS1, 8192, gate=("evA", 4))
        def g_exchange(cci, cco, ccsem, bnsem, gsem, cpname):
            WCP("g", cpname + "big", "sv")
            E("g", lambda g, cci=cci: g.dma_start(cci[:, :], BIG[:, :]), bnsem, 16)
            W("g", bnsem, 16)
            E("g", lambda g, cci=cci, cco=cco: g.collective_compute(
                "AllReduce", ALU.add,
                replica_groups=[[0, 1], [2, 3], [4, 5], [6, 7]],
                ins=[cci.ap().opt()], outs=[cco.ap().opt()]), ccsem, 1)
            W("g", ccsem, cnt[ccsem])
            E("g", lambda g, cco=cco, gsem=gsem: g.dma_start(G[:, :], cco[:, :]),
              gsem, 16)
            CP(cpname + "gath")

        WCP("g", "bt_pair", "sv")
        E("g", lambda g: g.dma_start(
            bass.AP(bthl_hbm, 0, [[1024, 2], [1, 1024]]),
            bass.AP(EB, 0, [[2048, 1], [1, 2048]])), "bto", 16)
        WCP("g", "prep", "sv")
        E("g", lambda g: g.dma_start(
            bass.AP(bbhl_hbm, 0, [[1024, 2], [1, 1024]]),
            bass.AP(EB, 0, [[2048, 1], [1, 2048]])), "bto", 16)
        W("g", "bto", 32)
        E("g", lambda g: g.dma_start(BTHL[0:2, :], bthl_hbm[:, :]), "bd", 16)
        E("g", lambda g: g.dma_start(BBHL[0:2, :], bbhl_hbm[:, :]), "bd", 16)
        CP("biasdma")
        W("g", "tr3", 64)
        wload("FW", "fw", WS0, 8192, gate=("evA", 32))
        g_exchange(ccqi_h, ccqo_h, "sccq", "bnq", "gq", "q_")
        wload("A1", "a1", WS1, 8192, gate=("mmA", 64))
        g_exchange(ccki_h, ccko_h, "scck", "bnk", "gk", "k_")
        wload("A2", "a2", WS0, 8192, gate=("mmA", 120))
        W("g", "mmA", 120)
        srcWT = bass.AP(w_hs["Wt"], 0, [[1024, 128], [131072, 8], [1, 1024]])
        dstWT = bass.AP(B2, 0, [[16384, 128], [1024, 8], [1, 1024]])
        E("g", lambda g: g.dma_start(dstWT, srcWT), "wt", 16)

        def x_transpose(i):
            WCP("sy", f"xn{i}", "sv")
            out_ap = bass.AP(B2, 128 * i, [[16384, P], [2048, 8], [1, P]])
            E("sy", lambda s, i=i, out_ap=out_ap: s.dma_start_transpose(
                out_ap, XN(i, 0, D)), f"tr{i // 4}", 16)

        for i in range(TCH):
            x_transpose(i)

        for k in range(TCH // 2):
            WCP("sy", f"fin_{2 * k + 1}", "sv")
            E("sy", lambda s, k=k: s.dma_start(
                out_h[128 * (2 * k + 1):128 * (2 * k + 2), :], OS[1]),
              "oo", 16)

        E("v", lambda v: v.memset(CSTS[:, 0:1], -0.5), "sv")
        E("v", lambda v: v.memset(CSTS[:, 1:2], EPS), "sv")
        E("v", lambda v: v.memset(CSTS[:, 2:3], SCALE), "sv")

        def XC(i, a, b):
            off = ((i // 2) % 2) * 2 * D + (i % 2) * D
            return H1B[:, off + a:off + b]

        def ln_stats_chunk(i):
            VW()
            E("v", lambda v, i=i: v.bn_stats(
                BNS[:, 12 * (i % 4):12 * (i % 4) + 6], XC(i, 0, 512)), "sv")
            E("v", lambda v, i=i: v.bn_stats(
                BNS[:, 12 * (i % 4) + 6:12 * (i % 4) + 12], XC(i, 512, 1024)),
              "sv")
            CP(f"bns{i}")
            if i % 4 == 3:
                g = i // 4
                VW()
                for k in range(4):
                    j = 4 * g + k
                    E("v", lambda v, j=j, k=k: v.bn_aggr(
                        MV[:, 2 * j:2 * j + 2], BNS[:, 12 * k:12 * k + 12]),
                      "sv")
                CP(f"st{g}")

        def ln_xn_group(g):
            WCP("v", f"rstd{g}", "ss")
            E("v", lambda v, g=g: v.scalar_tensor_tensor(
                NMRS4[:, :], bass.AP(MV, 8 * g, [[2 * TCH, P], [2, 4]]), -1.0,
                RSTD[:, 4 * g:4 * g + 4], ALU.mult, ALU.mult), "sv")
            VW()
            for k in range(4):
                i = 4 * g + k
                E("v", lambda v, i=i, k=k: v.tensor_scalar(
                    XN(i, 0, D), XC(i, 0, D), RSTD[:, i:i + 1],
                    NMRS4[:, k:k + 1], ALU.mult, ALU.add), "sv")
                CP(f"xn{i}")

        for g in range(4):
            x_cast(2 * g)
            x_cast(2 * g + 1)
            for i in range(4 * g, 4 * g + 4):
                ln_stats_chunk(i)
            ln_xn_group(g)

        E("v", lambda v: v.memset(XCHQ[:, :], 0.0), "sv")
        E("v", lambda v: v.memset(XCHK[:, :], 0.0), "sv")
        WCP("v", "consts", "cst")
        E("v", lambda v: v.tensor_scalar(BQAS[:, :], BQA[:, :], SCALE, None,
                                         ALU.mult), "sv")
        E("v", lambda v: v.tensor_scalar(BKAS[:, :], BKA[:, :], SCALE, None,
                                         ALU.mult), "sv")
        WCP("v", "b32", "b32")
        E("v", lambda v: v.tensor_copy(EB[0:1, 0:D], OSB[0:1, 0:D]), "sv")
        VW()
        E("v", lambda v: v.tensor_sub(EB[0:1, D:2 * D], OSB[0:1, 0:D],
                                      EB[0:1, 0:D]), "sv")
        CP("bt_pair")
        W("v", "bto", 16)
        E("v", lambda v: v.tensor_copy(EB[0:1, 0:D], OSB[0:1, D:2 * D]), "sv")
        VW()
        E("v", lambda v: v.tensor_sub(EB[0:1, D:2 * D], OSB[0:1, D:2 * D],
                                      EB[0:1, 0:D]), "sv")
        CP("prep")

        SCX = lambda half, a, b: XBB[32:48, half * 1024 + a: half * 1024 + b]

        def softmax_part(xch, cpbase):
            WCP("v", cpbase + "s", "ss")
            E("v", lambda v: v.tensor_max(SCX(0, 0, 1024), QSv[:, 0:1024],
                                          QSv[:, 1024:2048]), "sv")
            w, half = 512, 1
            while w >= 1:
                oh, ih = half, 1 - half
                VW()
                E("v", lambda v, w=w, oh=oh, ih=ih: v.tensor_max(
                    SCX(oh, 0, w), SCX(ih, 0, w), SCX(ih, w, 2 * w)), "sv")
                w //= 2
                half = 1 - half
            mx = SCX(1 - half, 0, 1)
            VW()
            E("v", lambda v, mx=mx: v.tensor_scalar(NEGM[:, :], mx, -1.0,
                                                    None, ALU.mult), "sv")
            E("v", lambda v, mx=mx, xch=xch: v.tensor_copy(xch[0:16, 8:9], mx),
              "sv")
            CP(cpbase + "negm")
            WCP("v", cpbase + "eb", "ss")
            E("v", lambda v, xch=xch: v.scalar_tensor_tensor(
                QSv[:, :], EB[:, :], 1.0, EB[:, :], ALU.mult, ALU.max,
                accum_out=xch[0:16, 9:10]), "sv")
            W("v", "sv", cnt["sv"])
            CP(cpbase + "sum")

        def v_pool(xch, srcf, cpname):
            base_mmB = cps[cpname + "erep0"]["mmB"] if not dry else 0
            idx = 0
            for c in range(NCH):
                for n in range(NT):
                    if not dry:
                        W("v", "mmB", base_mmB + idx + 1)
                    sl = (base_mmB + idx) % 4 if not dry else 0
                    E("v", lambda v, c=c, n=n, sl=sl, srcf=srcf:
                      v.scalar_tensor_tensor(
                          SCRL[:, 0:512], srcf(c, 512 * n, 512 * n + 512), 1.0,
                          psB[:, 512 * sl:512 * sl + 512], ALU.mult, ALU.mult,
                          accum_out=ACC4[:, n:n + 1]), "evB")
                    idx += 1
                W("v", "evB", cnt["evB"])
                VW()
                E("v", lambda v: v.tensor_add(M2S[:, :], ACC4[:, 0:1],
                                              ACC4[:, 1:2]), "sv")
                E("v", lambda v: v.tensor_add(NMRS[:, :], ACC4[:, 2:3],
                                              ACC4[:, 3:4]), "sv")
                VW()
                E("v", lambda v, c=c, xch=xch: v.tensor_add(
                    xch[:, c:c + 1], M2S[:, :], NMRS[:, :]), "sv")
            CP(cpname + "pool")

        def v_big(xch, cpname):
            VW()
            E("v", lambda v, xch=xch: v.tensor_tensor(
                bass.AP(BIG, 0, [[80, P], [10, 8], [1, 10]]),
                bass.AP(xch, 0, [[10, P], [0, 8], [1, 10]]),
                bass.AP(SLOT1, 0, [[8, P], [1, 8], [0, 10]]), ALU.mult), "sv")
            CP(cpname + "big")

        def v_merge(xch, peer, cpname, dstP):
            W("v", "gq" if cpname == "q_" else "gk", 16)
            for r in range(8):
                E("v", lambda v, r=r: v.tensor_scalar(
                    SCR3[:, 10 * r:10 * r + 10], G[:, 10 * r:10 * r + 10],
                    PEER1[:, r:r + 1], None, ALU.mult), "sv")
            VW()
            for p_ in range(4):
                E("v", lambda v, p_=p_: v.tensor_add(
                    TP4[:, 10 * p_:10 * p_ + 10],
                    SCR3[:, 20 * p_:20 * p_ + 10],
                    SCR3[:, 20 * p_ + 10:20 * p_ + 20]), "sv")
            VW()
            E("v", lambda v: v.tensor_add(SCR3[:, 0:10], TP4[:, 0:10],
                                          TP4[:, 10:20]), "sv")
            E("v", lambda v: v.tensor_add(SCR3[:, 10:20], TP4[:, 20:30],
                                          TP4[:, 30:40]), "sv")
            VW()
            E("v", lambda v, peer=peer: v.tensor_add(
                peer[:, :], SCR3[:, 0:10], SCR3[:, 10:20]), "sv")
            VW()
            E("v", lambda v, xch=xch, peer=peer: v.tensor_tensor(
                MRG[:, 0:1], xch[0:16, 8:9], peer[0:16, 8:9], ALU.max), "sv")
            VW()
            E("v", lambda v: v.tensor_scalar(MRG[:, 1:2], MRG[:, 0:1], -1.0,
                                             None, ALU.mult), "sv")
            CP(cpname + "negM")
            WCP("v", cpname + "fab", "ss")
            E("v", lambda v, xch=xch: v.tensor_mul(MRG[:, 4:5], xch[0:16, 9:10],
                                                   MRG[:, 2:3]), "sv")
            E("v", lambda v, peer=peer: v.tensor_mul(MRG[:, 5:6], peer[0:16, 9:10],
                                                     MRG[:, 3:4]), "sv")
            VW()
            E("v", lambda v: v.tensor_add(MRG[:, 6:7], MRG[:, 4:5], MRG[:, 5:6]), "sv")
            VW()
            E("v", lambda v: v.reciprocal(MRG[:, 7:8], MRG[:, 6:7]), "sv")
            VW()
            E("v", lambda v: v.tensor_mul(MRG[:, 8:9], MRG[:, 2:3], MRG[:, 7:8]), "sv")
            E("v", lambda v: v.tensor_mul(MRG[:, 9:10], MRG[:, 3:4], MRG[:, 7:8]), "sv")
            VW()
            E("v", lambda v: v.tensor_tensor(
                BAB[:, 0:8], bass.AP(MRG, 8, [[16, 16], [0, 8]]), MASKB[:, :],
                ALU.mult), "sv")
            E("v", lambda v: v.tensor_tensor(
                BAB[:, 8:16], bass.AP(MRG, 9, [[16, 16], [0, 8]]), MASKB[:, :],
                ALU.mult), "sv")
            CP(cpname + "bab")
            if not dry:
                W("v", "mmB", cps[cpname + "rep"]["mmB"])
            sl = (cps[cpname + "rep"]["mmB"] - 1) % 4 if not dry else 0
            E("v", lambda v, xch=xch, sl=sl: v.tensor_mul(
                PM1[:, :], xch[:, 0:8], psB[:, 512 * sl:512 * sl + 8]), "sv")
            E("v", lambda v, peer=peer, sl=sl: v.tensor_mul(
                PM2[:, :], peer[:, 0:8], psB[:, 512 * sl + 8:512 * sl + 16]),
              "evB")
            VW()
            E("v", lambda v, dstP=dstP: v.tensor_add(dstP[:, :], PM1[:, :],
                                                     PM2[:, :]), "sv")
            CP(cpname + "P")

        softmax_part(XCHQ, "q_")
        v_pool(XCHQ, MQ, "q_")
        v_big(XCHQ, "q_")
        v_merge(XCHQ, PEERQ, "q_", PQ)
        VW()
        for c in range(NCH):
            E("v", lambda v, c=c: v.tensor_scalar(
                WKAB[:, c * 16:(c + 1) * 16], WKAF[:, c * 16:(c + 1) * 16],
                PQ[:, c:c + 1], None, ALU.mult), "sv")
        CP("wkab")
        softmax_part(XCHK, "k_")
        v_pool(XCHK, MK, "k_")
        v_big(XCHK, "k_")
        v_merge(XCHK, PEERK, "k_", PMK)
        VW()
        E("v", lambda v: v.tensor_mul(PK[:, :], PQ[:, :], PMK[:, :]), "sv")
        VW()
        for c in range(NCH):
            E("v", lambda v, c=c: v.tensor_scalar(
                WV(c, 0, 2048), MQ(c, 0, 2048), PK[:, c:c + 1], None, ALU.mult),
              "sv")
        CP("wv")


        def s_rstd_group(g):
            WCP("s", f"st{g}", "sv")
            var4 = bass.AP(MV, 8 * g + 1, [[2 * TCH, P], [2, 4]])
            E("s", lambda s, var4=var4: s.activation(var4, var4, AF.Ln,
                                                     bias=CSTS[:, 1:2]), "ss")
            E("s", lambda s, g=g, var4=var4: s.activation(
                RSTD[:, 4 * g:4 * g + 4], var4, AF.Exp, scale=CSTS[:, 0:1]),
              "ss")
            CP(f"rstd{g}")

        for p in range(4):
            xf_load(p)
        s_rstd_group(0)
        s_rstd_group(1)
        for p in range(4, 8):
            xf_load(p)

        def s_late_ln(tid):
            if tid == 8:
                s_rstd_group(2)
            if tid == 16:
                s_rstd_group(3)
            if tid == 20:
                WCP("s", f"xn{TCH - 1}", "sv")
                E("s", lambda s: s.dma_start(OSB[0:1, 0:D], bt_h[:, :]),
                  "b32", 16)
                E("s", lambda s: s.dma_start(OSB[0:1, D:2 * D], bb_h[:, :]),
                  "b32", 16)
                CP("b32")

        def proj_tile(ws, m, movf, mov_a, dst_ap, bias_ap, pool=4):
            bi = cnt["mmA"] % pool
            bank = psA[bi] if bi < 4 else psB[:, 512 * (bi - 4):512 * (bi - 4) + 512]
            if cnt["mmA"] >= pool:
                W("t", "evA", cnt["mmA"] - (pool - 1))
            for k in range(NCH):
                E("t", lambda t, ws=ws, k=k, m=m, bank=bank:
                  t.matmul(bank[:, :] if not isinstance(bank, bass.AP) else bank,
                           WSL(ws, k, m * 128, (m + 1) * 128),
                           movf(k, mov_a, mov_a + 512),
                           start=(k == 0), stop=(k == NCH - 1)),
                  "mmA" if k == NCH - 1 else None)
            tid = cnt["mmA"]
            W("s", "mmA", tid)
            if tid == 1:
                W("s", "bqk", 32)
            s_late_ln(tid)
            bank_ap = bank if isinstance(bank, bass.AP) else bank[:, :]
            if bias_ap is None:
                E("s", lambda s, dst_ap=dst_ap, bank_ap=bank_ap:
                  s.activation(dst_ap, bank_ap, AF.Identity), "evA")
            else:
                E("s", lambda s, dst_ap=dst_ap, bank_ap=bank_ap, bias_ap=bias_ap:
                  s.activation(dst_ap, bank_ap, AF.Identity, bias=bias_ap),
                  "evA")

        W("t", "wq", 16)
        for wi in range(4):
            WCP("t", f"xn{wi}", "sv")
            for wj in range(10):
                E("t", lambda t: t.matmul(psB[:, 0:512], WS0[:, 0:128],
                                          WS0[:, 0:512], start=True, stop=True),
                  None)
        for wi in range(3):
            W("t", "tr0", 16 * (wi + 1))
            for wj in range(8):
                E("t", lambda t: t.matmul(psB[:, 0:512], WS0[:, 0:128],
                                          WS0[:, 0:512], start=True, stop=True),
                  None)

        for n in range(NT):
            W("t", f"tr{n}", 16 * 4)
            W("t", "wq", 16)
            for m in range(NCH):
                proj_tile(WS0, m, XNT, 512 * n, MQ(m, 512 * n, 512 * n + 512),
                          BQ[:, m:m + 1], pool=8)
        CP("step1")

        def mk_tiles(n_list, hook=None):
            W("t", "wk", 16)
            ti = 0
            for n in n_list:
                W("t", f"tr{n}", 16 * 4)
                for m in range(NCH):
                    proj_tile(WS1, m, XNT, 512 * n,
                              MK(m, 512 * n, 512 * n + 512), BK[:, m:m + 1])
                    if hook is not None:
                        hook(ti)
                    ti += 1

        def score_phase(stat_buf, movf, bias_sc, cpbase, ev_gate):
            for n in range(NT):
                if ev_gate is not None:
                    W("t", "evA", ev_gate(n))
                if cnt["mmB"] >= 4:
                    W("t", "evB", cnt["mmB"] - 3)
                for k in range(NCH):
                    E("t", lambda t, n=n, k=k, stat_buf=stat_buf:
                      t.matmul(psB[0:16, 512 * n:512 * n + 512],
                               stat_buf[:, k * 16:(k + 1) * 16],
                               movf(k, 512 * n, 512 * n + 512),
                               start=(k == 0), stop=(k == NCH - 1)),
                      "mmB" if k == NCH - 1 else None)
            W("s", "mmB", cnt["mmB"])
            WCP("s", "prep", "sv")
            WCP("s", f"xn{TCH - 1}", "sv")
            E("s", lambda s, bias_sc=bias_sc: s.activation(
                QSv, psB[0:16, :], AF.Identity, bias=bias_sc,
                scale=CSTS[0:16, 2:3]), "ss")
            E("s", lambda s: s.nop(), "evB", 4)
            CP(cpbase + "s")
            WCP("s", cpbase + "negm", "sv")
            if cpbase == "q_":
                W("s", "bd", 32)
            if cpbase == "k_":
                if not dry:
                    W("s", "mmB", cps["q_erep_done"]["mmB"])
            E("s", lambda s: s.activation(EB[:, :], QSv, AF.Exp,
                                          bias=NEGM[:, :]), "ss")
            CP(cpbase + "eb")

        def erep_group(cpname, c):
            if c == 0:
                WCP("t", cpname + "eb", "ss")
                CP(cpname + "erep0")
            for n in range(NT):
                sl = cnt["mmB"] % 4
                if cnt["mmB"] >= 4:
                    W("t", "evB", cnt["mmB"] - 3)
                E("t", lambda t, c=c, n=n, sl=sl:
                  t.matmul(psB[:, 512 * sl:512 * sl + 512],
                           SELP[:, c * 128:(c + 1) * 128],
                           EB[0:16, 512 * n:512 * n + 512],
                           start=True, stop=True), "mmB")
            if c == NCH - 1:
                CP(cpname + "erep_done")

        def rep_mms(cpname):
            WCP("t", cpname + "bab", "sv")
            sl = cnt["mmB"] % 4
            if cnt["mmB"] >= 4:
                W("t", "evB", cnt["mmB"] - 3)
            E("t", lambda t, sl=sl: t.matmul(psB[:, 512 * sl:512 * sl + 8],
                                             A16[:, :], BAB[:, 0:8],
                                             start=True, stop=True), None)
            E("t", lambda t, sl=sl: t.matmul(psB[:, 512 * sl + 8:512 * sl + 16],
                                             A16[:, :], BAB[:, 8:16],
                                             start=True, stop=True), "mmB")
            CP(cpname + "rep")

        def s_fab(cpname, xch, peer):
            WCP("s", cpname + "negM", "sv")
            E("s", lambda s, xch=xch: s.activation(MRG[:, 2:3], xch[0:16, 8:9],
                                                   AF.Exp, bias=MRG[0:16, 1:2]),
              "ss")
            E("s", lambda s, peer=peer: s.activation(MRG[:, 3:4], peer[0:16, 8:9],
                                                     AF.Exp, bias=MRG[0:16, 1:2]),
              "ss")
            CP(cpname + "fab")

        def h1_phase(n):
            W("t", "fw", 16)
            for m in range(NCH):
                proj_tile(WS0, m, XNT, 512 * n, H1V(m, 0, 512), None)

        def h2_phase(n, first, hook=None):
            W("t", "a1", 16)
            W("t", "evA", cnt["mmA"])
            for m in range(NCH):
                bank = psA[cnt["mmA"] % 4]
                if cnt["mmA"] >= 4:
                    W("t", "evA", cnt["mmA"] - 3)
                for k in range(NCH):
                    E("t", lambda t, k=k, m=m, bank=bank:
                      t.matmul(bank[:, :], WSL(WS1, k, m * 128, (m + 1) * 128),
                               H1V(k, 0, 512),
                               start=(k == 0), stop=(k == NCH - 1)),
                      "mmA" if k == NCH - 1 else None)
                tid = cnt["mmA"]
                W("s", "mmA", tid)
                if first and m == 0:
                    for bn in range(4):
                        W("s", f"tr{bn}", 16 * 4)
                E("s", lambda s, m=m, n=n, bank=bank: s.activation(
                    H2V(m, 512 * n, 512 * n + 512), bank[:, :], AF.Identity),
                  "evA")
                if hook is not None:
                    hook(m)

        W("t", "evA", 32)
        WCP("t", "consts", "cst")
        score_phase(WQAB, MQ, BQAS[:, :], "q_", None)
        mk_tiles([0])
        mk_tiles([1], hook=lambda ti: erep_group("q_", ti) if ti < 8 else None)
        mk_tiles([2, 3])
        h1_phase(0)
        s_fab("q_", XCHQ, PEERQ)
        rep_mms("q_")
        W("t", "evA", 64)
        WCP("t", "wkab", "sv")
        score_phase(WKAB, MK, BKAS[:, :], "k_", None)
        h2_phase(0, True)
        h1_phase(1)
        h2_phase(1, False,
                 hook=lambda m: erep_group("k_", m) if m < 8 else None)
        h1_phase(2)
        h2_phase(2, False)
        s_fab("k_", XCHK, PEERK)
        h1_phase(3)
        h2_phase(3, False)
        rep_mms("k_")
        CP("h2done")

        W("t", "bd", 32)
        bld = []
        for n in range(4):
            src_hl, j = (BTHL, BBHL)[n // 2], n % 2
            bank = psA[cnt["mmA"] % 4]
            if cnt["mmA"] >= 4:
                W("t", "evA", cnt["mmA"] - 3)
            E("t", lambda t, src_hl=src_hl, j=j, bank=bank:
              t.matmul(bank[:, :], ONES2[:, :], src_hl[:, 512 * j:512 * j + 512],
                       start=True, stop=True), "mmA")
            bld.append(cnt["mmA"])
        for n in range(4):
            bank = psA[(bld[n] - 1) % 4]
            W("v", "mmA", bld[n])
            E("v", lambda v, n=n, bank=bank: v.tensor_copy(
                WS1F[:, 512 * n:512 * n + 512], bank[:, :]), "sv")
            E("v", lambda v: v.nop(), "evA", 1)

        def x2_load(i):
            if i >= 2:
                WCP("g", f"fin_u1_{i - 2}", "sv")
            else:
                WCP("g", "k_eb", "ss")
                WCP("g", "k_sum", "sv")
            E("g", lambda g, i=i: g.dma_start(XBv(i, 0, D),
                                              x_h[128 * i:128 * (i + 1), :]),
              "x2e" if i % 2 == 0 else "x2o", 16)

        x2_load(0)
        x2_load(1)

        W("t", "a2", 16)
        W("t", "wt", 16)
        W("t", "bd", 32)
        WCP("t", "wv", "sv")
        for i in range(TCH):
            tids = []
            for j in range(2):
                bank = psA[cnt["mmA"] % 4]
                if cnt["mmA"] >= 4:
                    W("t", "evA", cnt["mmA"] - 3)
                for k in range(NCH):
                    E("t", lambda t, i=i, j=j, k=k, bank=bank:
                      t.matmul(bank[:, :], WV(k, 128 * i, 128 * i + 128),
                               WTV(k, 512 * j, 512 * j + 512),
                               start=(k == 0), stop=False), None)
                for q in range(4):
                    c = 4 * j + q
                    E("t", lambda t, i=i, c=c, q=q, bank=bank:
                      t.matmul(bank[:, 128 * q:128 * q + 128],
                               MQ(c, 128 * i, 128 * i + 128), EYE[:, :],
                               start=False, stop=(q == 3)),
                      "mmA" if q == 3 else None)
                tids.append(cnt["mmA"])
            for j in range(2):
                bank = psA[cnt["mmA"] % 4]
                if cnt["mmA"] >= 4:
                    W("t", "evA", cnt["mmA"] - 3)
                for k in range(NCH):
                    E("t", lambda t, i=i, j=j, k=k, bank=bank:
                      t.matmul(bank[:, :], H2V(k, 128 * i, 128 * i + 128),
                               WSL(WS0, k, 512 * j, 512 * j + 512),
                               start=(k == 0), stop=(k == NCH - 1)),
                      "mmA" if k == NCH - 1 else None)
                tids.append(cnt["mmA"])

            FBV = [WS1F[:, 2048 + 512 * jj:2560 + 512 * jj] for jj in range(2)]
            U1B = [WS1F[:, 3072 + 512 * jj:3584 + 512 * jj] for jj in range(2)]
            for j in range(2):
                W("v", "mmA", tids[j])
                if j == 0:
                    W("v", "x2e" if i % 2 == 0 else "x2o", 16 * (i // 2 + 1))
                    if i >= 2:
                        W("v", "oe" if i % 2 == 0 else "oo",
                          16 * ((i - 2) // 2 + 1))
                    if i == 0:
                        VW()
                E("v", lambda v, i=i, j=j, a_bank=psA[(tids[j] - 1) % 4]:
                  v.tensor_add(U1B[j], a_bank[:, :],
                               XBv(i, 512 * j, 512 * j + 512)), "sv")
                E("v", lambda v: v.nop(), "evA", 1)
            CP(f"fin_u1_{i}")
            for j in range(2):
                W("v", "mmA", tids[2 + j])
                E("v", lambda v, j=j, f_bank=psA[(tids[2 + j] - 1) % 4]:
                  v.tensor_add(FBV[j], f_bank[:, :],
                               WS1F[:, 1024 + 512 * j:1536 + 512 * j]), "sv")
                E("v", lambda v: v.nop(), "evA", 1)
                CP(f"fin_fb_{i}_{j}")

            for j in range(2):
                WCP("s", f"fin_fb_{i}_{j}", "sv")
                if i >= 1:
                    WCP("s", f"fin_og_{i - 1}_{j}", "sv")
                E("s", lambda s, j=j: s.activation(
                    T2[j][:, :], FBV[j][:, :], AF.Tanh), "ss")
                CP(f"fin_t2_{i}_{j}")

            VW()
            for j in range(2):
                E("v", lambda v, j=j: v.tensor_add(
                    U1B[j], U1B[j], WS1F[:, 512 * j:512 * j + 512]), "sv")
            for j in range(2):
                WCP("v", f"fin_t2_{i}_{j}", "ss")
                VW()
                E("v", lambda v, i=i, j=j: v.tensor_add(
                    OSB[:, (i % 2) * D + 512 * j:(i % 2) * D + 512 * j + 512],
                    U1B[j], T2[j][:, :]), "sv")
                CP(f"fin_og_{i}_{j}")
            CP(f"fin_{i}")

            if i % 2 == 0:
                WCP("s", f"fin_{i}", "sv")
                E("s", lambda s, i=i: s.dma_start(
                    out_h[128 * i:128 * (i + 1), :], OS[i % 2]), "oe", 16)
            if i + 2 < TCH:
                x2_load(i + 2)

        W("s", "oe", 16 * (TCH // 2))
        W("s", "oo", 16 * (TCH // 2))
        return cnt

    final_counts = walk("dry", None, True)
    for nm, snap in cps.items():
        for k, v in snap.items():
            assert v <= final_counts[k], (nm, k, v, final_counts[k])

    with nc.Block() as block:
        @block.gpsimd
        def _(g):
            walk("g", g, False)

        @block.sync
        def _(sy):
            walk("sy", sy, False)

        @block.vector
        def _(v):
            walk("v", v, False)

        @block.scalar
        def _(s):
            walk("s", s, False)

        @block.tensor
        def _(t):
            walk("t", t, False)

    es.close()
    return nc


def make_in_maps(g):
    x = np.asarray(g["L0_attn_hidden_states"], dtype=np.float32)
    Wq = np.asarray(g["L0_attn_Wq"], np.float32)
    Wk = np.asarray(g["L0_attn_Wk"], np.float32)
    Wt = np.asarray(g["L0_attn_Wt"], np.float32)
    FW = np.asarray(g["L0_ffn_weight"], np.float32)
    A1 = np.asarray(g["L0_ffn_aux_weight1"], np.float32)
    A2 = np.asarray(g["L0_ffn_aux_weight2"], np.float32)
    Wqa = np.asarray(g["L0_attn_Wqa"], np.float32)
    Wka = np.asarray(g["L0_attn_Wka"], np.float32)
    bq = np.asarray(g["L0_attn_bq"], np.float32)
    bk = np.asarray(g["L0_attn_bk"], np.float32)
    bqa = np.asarray(g["L0_attn_bqa"], np.float32)
    bka = np.asarray(g["L0_attn_bka"], np.float32)
    bt = np.asarray(g["L0_attn_bt"], np.float32)
    bb = np.asarray(g["L0_ffn_bias"], np.float32)

    C = np.ascontiguousarray
    wqa_l = C(Wqa.reshape(8, 128, 16).transpose(1, 0, 2).reshape(128, 128))
    wka_l = C(Wka.reshape(8, 128, 16).transpose(1, 0, 2).reshape(128, 128))
    bq_l = C(bq.reshape(8, 128).T)
    bk_l = C(bk.reshape(8, 128).T)
    bqa_l = C(bqa.reshape(16, 1))
    bka_l = C(bka.reshape(16, 1))
    bt_l = C(bt.reshape(1, 1024))
    bb_l = C(bb.reshape(1, 1024))

    selp = np.zeros((16, 8, 128), np.float32)
    for c in range(8):
        selp[2 * c, c, 0:64] = 1.0
        selp[2 * c + 1, c, 64:128] = 1.0
    selp = C(selp.reshape(16, 1024).astype(BF))
    a16 = np.zeros((16, 128), np.float32)
    a16[0::2, 0:64] = 1.0
    a16[1::2, 64:128] = 1.0
    a16 = C(a16.astype(BF))
    maskb = np.zeros((16, 8), np.float32)
    for h in range(16):
        maskb[h, h // 2] = 1.0
    eye = C(np.eye(128, dtype=np.float32).astype(BF))
    ones2 = C(np.ones((2, 128), np.float32).astype(BF))

    in_maps = []
    for c in range(8):
        b, hf = c // 2, c % 2
        slab = C(x[b, hf * 2048:(hf + 1) * 2048, :])
        slot1 = np.zeros((128, 8), np.float32); slot1[:, c] = 1.0
        peer1 = np.zeros((128, 8), np.float32); peer1[:, c ^ 1] = 1.0
        in_maps.append({
            "x": slab, "Wq": Wq, "Wk": Wk, "Wt": Wt, "FW": FW, "A1": A1,
            "A2": A2, "WqaL": wqa_l, "WkaL": wka_l, "bqL": bq_l, "bkL": bk_l,
            "bqaL": bqa_l, "bkaL": bka_l, "btL": bt_l, "bbL": bb_l,
            "selp": selp, "a16": a16, "maskb": maskb, "eye": eye,
            "ones2": ones2, "slot1": slot1, "peer1": peer1,
        })
    return in_maps


def kernel(**inputs):
    g = {k: np.asarray(v) for k, v in inputs.items()}
    in_maps = make_in_maps(g)
    nc = build_nc()
    res = run_bass_kernel_spmd(nc, in_maps, core_ids=list(range(8)))
    out = np.empty((4, 4096, 1024), np.float32)
    for c in range(8):
        b, hf = c // 2, c % 2
        out[b, hf * 2048:(hf + 1) * 2048, :] = res.results[c]["out"]
    return out


# version 35 (final)
# speedup vs baseline: 1.2018x; 1.0362x over previous
import numpy as np
import ml_dtypes
import concourse.bass as bass
import concourse.mybir as mybir
from concourse.bass_utils import run_bass_kernel_spmd
from contextlib import ExitStack

F32 = mybir.dt.float32
BF16 = mybir.dt.bfloat16
AF = mybir.ActivationFunctionType
ALU = mybir.AluOpType
AX = mybir.AxisListType

BF = ml_dtypes.bfloat16
P = 128
T = 2048
D = 1024
H = 16
NCH = 8
TCH = 16
NT = 4
SCALE = 0.125
EPS = 1e-5


def build_nc():
    nc = bass.Bass("TRN2", target_bir_lowering=False)

    x_h = nc.dram_tensor("x", [T, D], F32, kind="ExternalInput")
    out_h = nc.dram_tensor("out", [T, D], F32, kind="ExternalOutput")
    w_hs = {}
    for nm in ("Wq", "Wk", "Wt", "FW", "A1", "A2"):
        w_hs[nm] = nc.dram_tensor(nm, [D, D], F32, kind="ExternalInput")
    wqa_h = nc.dram_tensor("WqaL", [P, P], F32, kind="ExternalInput")
    wka_h = nc.dram_tensor("WkaL", [P, P], F32, kind="ExternalInput")
    bq_h = nc.dram_tensor("bqL", [P, 8], F32, kind="ExternalInput")
    bk_h = nc.dram_tensor("bkL", [P, 8], F32, kind="ExternalInput")
    bqa_h = nc.dram_tensor("bqaL", [H, 1], F32, kind="ExternalInput")
    bka_h = nc.dram_tensor("bkaL", [H, 1], F32, kind="ExternalInput")
    bt_h = nc.dram_tensor("btL", [1, D], F32, kind="ExternalInput")
    bb_h = nc.dram_tensor("bbL", [1, D], F32, kind="ExternalInput")
    selp_h = nc.dram_tensor("selp", [H, 1024], BF16, kind="ExternalInput")
    a16_h = nc.dram_tensor("a16", [H, P], BF16, kind="ExternalInput")
    maskb_h = nc.dram_tensor("maskb", [H, 8], F32, kind="ExternalInput")
    eye_h = nc.dram_tensor("eye", [P, P], BF16, kind="ExternalInput")
    ones2_h = nc.dram_tensor("ones2", [2, P], BF16, kind="ExternalInput")
    slot1_h = nc.dram_tensor("slot1", [P, 8], F32, kind="ExternalInput")
    peer1_h = nc.dram_tensor("peer1", [P, 8], F32, kind="ExternalInput")
    ccqi_h = nc.dram_tensor("ccqi", [P, 80], F32)
    ccqo_h = nc.dram_tensor("ccqo", [P, 80], F32)
    ccki_h = nc.dram_tensor("ccki", [P, 80], F32)
    ccko_h = nc.dram_tensor("ccko", [P, 80], F32)
    bthl_hbm = nc.dram_tensor("bthl_hbm", [2, D], BF16)
    bbhl_hbm = nc.dram_tensor("bbhl_hbm", [2, D], BF16)

    es = ExitStack()
    sb = lambda nm, shp, dt: es.enter_context(nc.sbuf_tensor(nm, shp, dt))
    psm = lambda nm, shp: es.enter_context(nc.psum_tensor(nm, shp, F32))

    B1 = sb("B1", [P, 16384], BF16)
    B2 = sb("B2", [P, 16384], BF16)
    B3 = sb("B3", [P, 16384], BF16)
    B4 = sb("B4", [P, 16384], BF16)
    H1B = sb("H1B", [P, 4096], BF16)
    WS0 = sb("WS0", [P, 8192], BF16)
    WS1 = sb("WS1", [P, 8192], BF16)
    WS1F = WS1.bitcast(F32)
    XBB = sb("XBB", [P, 2 * D], F32)
    OSB = sb("OSB", [P, 2 * D], F32)
    OS = [OSB[:, 0:D], OSB[:, D:2 * D]]
    U1 = sb("U1", [P, 512], F32)
    T2 = [sb(f"T2{i}", [P, 512], F32) for i in range(2)]
    SCRL = sb("SCRL", [P, 512], BF16)
    EB = sb("EB", [H, T], BF16)
    BNS = sb("BNS", [P, 48], F32)
    MV = sb("MV", [P, 2 * TCH], F32)
    NMRS4 = sb("NMRS4", [P, 4], F32)
    RSTD = sb("RSTD", [P, TCH], F32)
    M2S = sb("M2S", [P, 1], F32)
    NMRS = sb("NMRS", [P, 1], F32)
    WQAB = sb("WQAB", [P, P], BF16)
    WKAF = sb("WKAF", [P, P], F32)
    WKAB = sb("WKAB", [P, P], BF16)
    BQ = sb("BQ", [P, 8], F32)
    BK = sb("BK", [P, 8], F32)
    BQA = sb("BQA", [H, 1], F32)
    BKA = sb("BKA", [H, 1], F32)
    BQAS = sb("BQAS", [H, 1], F32)
    BKAS = sb("BKAS", [H, 1], F32)
    BTHL = sb("BTHL", [2, D], BF16)
    BBHL = sb("BBHL", [2, D], BF16)
    SELP = sb("SELP", [H, 1024], BF16)
    A16 = sb("A16", [H, P], BF16)
    MASKB = sb("MASKB", [H, 8], F32)
    EYE = sb("EYE", [P, P], BF16)
    ONES2 = sb("ONES2", [2, P], BF16)
    SLOT1 = sb("SLOT1", [P, 8], F32)
    PEER1 = sb("PEER1", [P, 8], F32)
    XCHQ = sb("XCHQ", [P, 10], F32)
    XCHK = sb("XCHK", [P, 10], F32)
    BIG = sb("BIG", [P, 80], F32)
    G = sb("G", [P, 80], F32)
    PEERQ = sb("PEERQ", [P, 10], F32)
    PEERK = sb("PEERK", [P, 10], F32)
    NEGM = sb("NEGM", [H, 1], F32)
    MRG = sb("MRG", [H, 16], F32)
    BAB = sb("BAB", [H, 16], BF16)
    PM1 = sb("PM1", [P, 8], F32)
    PM2 = sb("PM2", [P, 8], F32)
    PQ = sb("PQ", [P, 8], F32)
    PMK = sb("PMK", [P, 8], F32)
    PK = sb("PK", [P, 8], F32)
    ACC4 = sb("ACC4", [P, 4], F32)
    CSTS = sb("CSTS", [P, 4], F32)
    TP4 = sb("TP4", [P, 40], F32)

    psA = [psm(f"psA{i}", [P, 512]) for i in range(4)]
    psB = psm("psB", [P, T])

    sems = {}
    for nm in ("cst", "bqk", "wq", "wk", "fw", "a1", "a2", "wt", "bto", "bd", "b32",
               "bnq", "bnk", "gq", "gk", "x0", "x1", "x2", "x3", "x2e", "x2o", "oe", "oo",
               "tr0", "tr1", "tr2", "tr3", "sv", "ss",
               "mmA", "evA", "mmB", "evB", "sccq", "scck", "xf0", "xf1"):
        sems[nm] = es.enter_context(nc.semaphore(nm))

    cps = {}

    def walk(active, eng, dry):
        cnt = {k: 0 for k in sems}

        def bump(sem, k):
            cnt[sem] += k
            return cnt[sem]

        def CP(name):
            if dry:
                cps[name] = dict(cnt)

        def W(tag, sem, val):
            if not dry and active == tag and val > 0:
                eng.wait_ge(sems[sem], val)

        def WCP(tag, name, sem):
            if not dry:
                W(tag, sem, cps[name][sem])

        def VW():
            W("v", "sv", cnt["sv"])

        def E(tag, f, inc=None, inck=1):
            if not dry and active == tag:
                ins = f(eng)
                if inc is not None:
                    ins.then_inc(sems[inc], inck)
            if inc is not None:
                bump(inc, inck)

        SCR3 = BIG
        XBv = lambda i, a, b: XBB[:, (i % 2) * D + a: (i % 2) * D + b]
        QSv = XBB[0:16, :]
        XN = lambda i, a, b: B1[:, i * 1024 + a: i * 1024 + b]
        H2V = lambda c, a, b: B1[:, c * 2048 + a: c * 2048 + b]
        XNT = lambda c, a, b: B2[:, c * 2048 + a: c * 2048 + b]
        WTV = lambda k, a, b: B2[:, k * 1024 + a: k * 1024 + b]
        MQ = lambda c, a, b: B3[:, c * 2048 + a: c * 2048 + b]
        MK = lambda c, a, b: B4[:, c * 2048 + a: c * 2048 + b]
        WV = lambda c, a, b: B4[:, c * 2048 + a: c * 2048 + b]
        H1V = lambda k, a, b: H1B[:, k * 512 + a: k * 512 + b]
        WSL = lambda ws, k, a, b: ws[:, k * 1024 + a: k * 1024 + b]

        def x_pair_load(p):
            if p >= 2:
                WCP("g", f"xn{2 * (p - 2) + 1}", "sv")
            dst = bass.AP(H1B, (p % 2) * 2 * D, [[4 * D, P], [D, 2], [1, D]])
            src_ = bass.AP(x_h, 256 * p * D, [[D, P], [P * D, 2], [1, D]])
            E("g", lambda g, dst=dst, src_=src_: g.dma_start(dst, src_),
              f"x{p % 2}", 16)

        def xf_load(p):
            stg = XBB if p % 2 == 0 else OSB
            if p >= 2:
                WCP("s", f"xcast{p - 2}", "sv")
            dst = bass.AP(stg, 0, [[2 * D, P], [D, 2], [1, D]])
            src_ = bass.AP(x_h, 256 * p * D, [[D, P], [P * D, 2], [1, D]])
            E("s", lambda s, dst=dst, src_=src_: s.dma_start(dst, src_),
              f"xf{p % 2}", 16)

        def x_cast(p):
            stg = XBB if p % 2 == 0 else OSB
            W("v", f"xf{p % 2}", 16 * (p // 2 + 1))
            if p >= 2:
                WCP("v", f"xn{2 * (p - 2) + 1}", "sv")
            E("v", lambda v, p=p, stg=stg: v.tensor_copy(
                H1B[:, (p % 2) * 2 * D:(p % 2) * 2 * D + 2 * D], stg[:, :]),
              "sv")
            CP(f"xcast{p}")


        E("g", lambda g: g.dma_start(BQ[:, :], bq_h[:, :]), "bqk", 16)
        E("g", lambda g: g.dma_start(BK[:, :], bk_h[:, :]), "bqk", 16)

        def wload(name, sem, dst_handle, dst_row_elems, gate=None):
            if gate is not None:
                W("g", gate[0], gate[1])
            src = bass.AP(w_hs[name], 0, [[1024, 128], [131072, 8], [1, 1024]])
            dst = bass.AP(dst_handle, 0, [[dst_row_elems, 128], [1024, 8], [1, 1024]])
            E("g", lambda g: g.dma_start(dst, src), sem, 16)

        wload("Wq", "wq", WS0, 8192)
        for (dst, src) in ((SELP[:, :], selp_h[:, :]), (A16[:, :], a16_h[:, :]),
                           (MASKB[:, :], maskb_h[:, :]), (EYE[:, :], eye_h[:, :]),
                           (ONES2[:, :], ones2_h[:, :]), (SLOT1[:, :], slot1_h[:, :]),
                           (PEER1[:, :], peer1_h[:, :]), (BQA[:, :], bqa_h[:, :]),
                           (BKA[:, :], bka_h[:, :]), (WQAB[:, :], wqa_h[:, :]),
                           (WKAF[:, :], wka_h[:, :])):
            E("g", lambda g, d=dst, s=src: g.dma_start(d, s), "cst", 16)
        CP("consts")
        wload("Wk", "wk", WS1, 8192, gate=("evA", 4))
        def g_exchange(cci, cco, ccsem, bnsem, gsem, cpname):
            WCP("g", cpname + "big", "sv")
            E("g", lambda g, cci=cci: g.dma_start(cci[:, :], BIG[:, :]), bnsem, 16)
            W("g", bnsem, 16)
            E("g", lambda g, cci=cci, cco=cco: g.collective_compute(
                "AllReduce", ALU.add,
                replica_groups=[[0, 1], [2, 3], [4, 5], [6, 7]],
                ins=[cci.ap().opt()], outs=[cco.ap().opt()]), ccsem, 1)
            W("g", ccsem, cnt[ccsem])
            E("g", lambda g, cco=cco, gsem=gsem: g.dma_start(G[:, :], cco[:, :]),
              gsem, 16)
            CP(cpname + "gath")

        WCP("g", "bt_pair", "sv")
        E("g", lambda g: g.dma_start(
            bass.AP(bthl_hbm, 0, [[1024, 2], [1, 1024]]),
            bass.AP(EB, 0, [[2048, 1], [1, 2048]])), "bto", 16)
        WCP("g", "prep", "sv")
        E("g", lambda g: g.dma_start(
            bass.AP(bbhl_hbm, 0, [[1024, 2], [1, 1024]]),
            bass.AP(EB, 0, [[2048, 1], [1, 2048]])), "bto", 16)
        W("g", "bto", 32)
        E("g", lambda g: g.dma_start(BTHL[0:2, :], bthl_hbm[:, :]), "bd", 16)
        E("g", lambda g: g.dma_start(BBHL[0:2, :], bbhl_hbm[:, :]), "bd", 16)
        CP("biasdma")
        W("g", "tr3", 64)
        wload("FW", "fw", WS0, 8192, gate=("evA", 32))
        g_exchange(ccqi_h, ccqo_h, "sccq", "bnq", "gq", "q_")
        wload("A1", "a1", WS1, 8192, gate=("mmA", 64))
        g_exchange(ccki_h, ccko_h, "scck", "bnk", "gk", "k_")
        W("g", "mmA", 120)
        for j in range(2):
            srcWT = bass.AP(w_hs["Wt"], 512 * j,
                            [[1024, 128], [131072, 8], [1, 512]])
            dstWT = bass.AP(B2, 512 * j, [[16384, 128], [1024, 8], [1, 512]])
            E("g", lambda g, srcWT=srcWT, dstWT=dstWT:
              g.dma_start(dstWT, srcWT), "wt", 16)
        for j in range(2):
            srcA2 = bass.AP(w_hs["A2"], 512 * j,
                            [[1024, 128], [131072, 8], [1, 512]])
            dstA2 = bass.AP(WS0, 512 * j, [[8192, 128], [1024, 8], [1, 512]])
            E("g", lambda g, srcA2=srcA2, dstA2=dstA2:
              g.dma_start(dstA2, srcA2), "a2", 16)

        def x_transpose(i):
            WCP("sy", f"xn{i}", "sv")
            out_ap = bass.AP(B2, 128 * i, [[16384, P], [2048, 8], [1, P]])
            E("sy", lambda s, i=i, out_ap=out_ap: s.dma_start_transpose(
                out_ap, XN(i, 0, D)), f"tr{i // 4}", 16)

        for i in range(TCH):
            x_transpose(i)

        for k in range(TCH // 2):
            WCP("sy", f"fin_{2 * k + 1}", "sv")
            E("sy", lambda s, k=k: s.dma_start(
                out_h[128 * (2 * k + 1):128 * (2 * k + 2), :], OS[1]),
              "oo", 16)

        E("v", lambda v: v.memset(CSTS[:, 0:1], -0.5), "sv")
        E("v", lambda v: v.memset(CSTS[:, 1:2], EPS), "sv")
        E("v", lambda v: v.memset(CSTS[:, 2:3], SCALE), "sv")

        def XC(i, a, b):
            off = ((i // 2) % 2) * 2 * D + (i % 2) * D
            return H1B[:, off + a:off + b]

        def ln_stats_chunk(i):
            VW()
            E("v", lambda v, i=i: v.bn_stats(
                BNS[:, 12 * (i % 4):12 * (i % 4) + 6], XC(i, 0, 512)), "sv")
            E("v", lambda v, i=i: v.bn_stats(
                BNS[:, 12 * (i % 4) + 6:12 * (i % 4) + 12], XC(i, 512, 1024)),
              "sv")
            CP(f"bns{i}")
            if i % 4 == 3:
                g = i // 4
                VW()
                for k in range(4):
                    j = 4 * g + k
                    E("v", lambda v, j=j, k=k: v.bn_aggr(
                        MV[:, 2 * j:2 * j + 2], BNS[:, 12 * k:12 * k + 12]),
                      "sv")
                CP(f"st{g}")

        def ln_xn_group(g):
            WCP("v", f"rstd{g}", "ss")
            E("v", lambda v, g=g: v.scalar_tensor_tensor(
                NMRS4[:, :], bass.AP(MV, 8 * g, [[2 * TCH, P], [2, 4]]), -1.0,
                RSTD[:, 4 * g:4 * g + 4], ALU.mult, ALU.mult), "sv")
            VW()
            for k in range(4):
                i = 4 * g + k
                E("v", lambda v, i=i, k=k: v.tensor_scalar(
                    XN(i, 0, D), XC(i, 0, D), RSTD[:, i:i + 1],
                    NMRS4[:, k:k + 1], ALU.mult, ALU.add), "sv")
                CP(f"xn{i}")

        for g in range(4):
            x_cast(2 * g)
            x_cast(2 * g + 1)
            for i in range(4 * g, 4 * g + 4):
                ln_stats_chunk(i)
            ln_xn_group(g)

        E("v", lambda v: v.memset(XCHQ[:, :], 0.0), "sv")
        E("v", lambda v: v.memset(XCHK[:, :], 0.0), "sv")
        WCP("v", "consts", "cst")
        E("v", lambda v: v.tensor_scalar(BQAS[:, :], BQA[:, :], SCALE, None,
                                         ALU.mult), "sv")
        E("v", lambda v: v.tensor_scalar(BKAS[:, :], BKA[:, :], SCALE, None,
                                         ALU.mult), "sv")
        WCP("v", "b32", "b32")
        E("v", lambda v: v.tensor_copy(EB[0:1, 0:D], OSB[0:1, 0:D]), "sv")
        VW()
        E("v", lambda v: v.tensor_sub(EB[0:1, D:2 * D], OSB[0:1, 0:D],
                                      EB[0:1, 0:D]), "sv")
        CP("bt_pair")
        W("v", "bto", 16)
        E("v", lambda v: v.tensor_copy(EB[0:1, 0:D], OSB[0:1, D:2 * D]), "sv")
        VW()
        E("v", lambda v: v.tensor_sub(EB[0:1, D:2 * D], OSB[0:1, D:2 * D],
                                      EB[0:1, 0:D]), "sv")
        CP("prep")

        SCX = lambda half, a, b: XBB[32:48, half * 1024 + a: half * 1024 + b]

        def softmax_part(xch, cpbase):
            WCP("v", cpbase + "s", "ss")
            E("v", lambda v: v.tensor_max(SCX(0, 0, 1024), QSv[:, 0:1024],
                                          QSv[:, 1024:2048]), "sv")
            w, half = 512, 1
            while w >= 1:
                oh, ih = half, 1 - half
                VW()
                E("v", lambda v, w=w, oh=oh, ih=ih: v.tensor_max(
                    SCX(oh, 0, w), SCX(ih, 0, w), SCX(ih, w, 2 * w)), "sv")
                w //= 2
                half = 1 - half
            mx = SCX(1 - half, 0, 1)
            VW()
            E("v", lambda v, mx=mx: v.tensor_scalar(NEGM[:, :], mx, -1.0,
                                                    None, ALU.mult), "sv")
            E("v", lambda v, mx=mx, xch=xch: v.tensor_copy(xch[0:16, 8:9], mx),
              "sv")
            CP(cpbase + "negm")
            WCP("v", cpbase + "eb", "ss")
            E("v", lambda v, xch=xch: v.scalar_tensor_tensor(
                QSv[:, :], EB[:, :], 1.0, EB[:, :], ALU.mult, ALU.max,
                accum_out=xch[0:16, 9:10]), "sv")
            W("v", "sv", cnt["sv"])
            CP(cpbase + "sum")

        def v_pool(xch, srcf, cpname):
            base_mmB = cps[cpname + "erep0"]["mmB"] if not dry else 0
            idx = 0
            for c in range(NCH):
                for n in range(NT):
                    if not dry:
                        W("v", "mmB", base_mmB + idx + 1)
                    sl = (base_mmB + idx) % 4 if not dry else 0
                    E("v", lambda v, c=c, n=n, sl=sl, srcf=srcf:
                      v.scalar_tensor_tensor(
                          SCRL[:, 0:512], srcf(c, 512 * n, 512 * n + 512), 1.0,
                          psB[:, 512 * sl:512 * sl + 512], ALU.mult, ALU.mult,
                          accum_out=ACC4[:, n:n + 1]), "evB")
                    idx += 1
                W("v", "evB", cnt["evB"])
                VW()
                E("v", lambda v: v.tensor_add(M2S[:, :], ACC4[:, 0:1],
                                              ACC4[:, 1:2]), "sv")
                E("v", lambda v: v.tensor_add(NMRS[:, :], ACC4[:, 2:3],
                                              ACC4[:, 3:4]), "sv")
                VW()
                E("v", lambda v, c=c, xch=xch: v.tensor_add(
                    xch[:, c:c + 1], M2S[:, :], NMRS[:, :]), "sv")
            CP(cpname + "pool")

        def v_big(xch, cpname):
            VW()
            E("v", lambda v, xch=xch: v.tensor_tensor(
                bass.AP(BIG, 0, [[80, P], [10, 8], [1, 10]]),
                bass.AP(xch, 0, [[10, P], [0, 8], [1, 10]]),
                bass.AP(SLOT1, 0, [[8, P], [1, 8], [0, 10]]), ALU.mult), "sv")
            CP(cpname + "big")

        def v_merge(xch, peer, cpname, dstP):
            W("v", "gq" if cpname == "q_" else "gk", 16)
            for r in range(8):
                E("v", lambda v, r=r: v.tensor_scalar(
                    SCR3[:, 10 * r:10 * r + 10], G[:, 10 * r:10 * r + 10],
                    PEER1[:, r:r + 1], None, ALU.mult), "sv")
            VW()
            for p_ in range(4):
                E("v", lambda v, p_=p_: v.tensor_add(
                    TP4[:, 10 * p_:10 * p_ + 10],
                    SCR3[:, 20 * p_:20 * p_ + 10],
                    SCR3[:, 20 * p_ + 10:20 * p_ + 20]), "sv")
            VW()
            E("v", lambda v: v.tensor_add(SCR3[:, 0:10], TP4[:, 0:10],
                                          TP4[:, 10:20]), "sv")
            E("v", lambda v: v.tensor_add(SCR3[:, 10:20], TP4[:, 20:30],
                                          TP4[:, 30:40]), "sv")
            VW()
            E("v", lambda v, peer=peer: v.tensor_add(
                peer[:, :], SCR3[:, 0:10], SCR3[:, 10:20]), "sv")
            VW()
            E("v", lambda v, xch=xch, peer=peer: v.tensor_tensor(
                MRG[:, 0:1], xch[0:16, 8:9], peer[0:16, 8:9], ALU.max), "sv")
            VW()
            E("v", lambda v: v.tensor_scalar(MRG[:, 1:2], MRG[:, 0:1], -1.0,
                                             None, ALU.mult), "sv")
            CP(cpname + "negM")
            WCP("v", cpname + "fab", "ss")
            E("v", lambda v, xch=xch: v.tensor_mul(MRG[:, 4:5], xch[0:16, 9:10],
                                                   MRG[:, 2:3]), "sv")
            E("v", lambda v, peer=peer: v.tensor_mul(MRG[:, 5:6], peer[0:16, 9:10],
                                                     MRG[:, 3:4]), "sv")
            VW()
            E("v", lambda v: v.tensor_add(MRG[:, 6:7], MRG[:, 4:5], MRG[:, 5:6]), "sv")
            VW()
            E("v", lambda v: v.reciprocal(MRG[:, 7:8], MRG[:, 6:7]), "sv")
            VW()
            E("v", lambda v: v.tensor_mul(MRG[:, 8:9], MRG[:, 2:3], MRG[:, 7:8]), "sv")
            E("v", lambda v: v.tensor_mul(MRG[:, 9:10], MRG[:, 3:4], MRG[:, 7:8]), "sv")
            VW()
            E("v", lambda v: v.tensor_tensor(
                BAB[:, 0:8], bass.AP(MRG, 8, [[16, 16], [0, 8]]), MASKB[:, :],
                ALU.mult), "sv")
            E("v", lambda v: v.tensor_tensor(
                BAB[:, 8:16], bass.AP(MRG, 9, [[16, 16], [0, 8]]), MASKB[:, :],
                ALU.mult), "sv")
            CP(cpname + "bab")
            if not dry:
                W("v", "mmB", cps[cpname + "rep"]["mmB"])
            sl = (cps[cpname + "rep"]["mmB"] - 1) % 4 if not dry else 0
            E("v", lambda v, xch=xch, sl=sl: v.tensor_mul(
                PM1[:, :], xch[:, 0:8], psB[:, 512 * sl:512 * sl + 8]), "sv")
            E("v", lambda v, peer=peer, sl=sl: v.tensor_mul(
                PM2[:, :], peer[:, 0:8], psB[:, 512 * sl + 8:512 * sl + 16]),
              "evB")
            VW()
            E("v", lambda v, dstP=dstP: v.tensor_add(dstP[:, :], PM1[:, :],
                                                     PM2[:, :]), "sv")
            CP(cpname + "P")

        softmax_part(XCHQ, "q_")
        v_pool(XCHQ, MQ, "q_")
        v_big(XCHQ, "q_")
        v_merge(XCHQ, PEERQ, "q_", PQ)
        VW()
        for c in range(NCH):
            E("v", lambda v, c=c: v.tensor_scalar(
                WKAB[:, c * 16:(c + 1) * 16], WKAF[:, c * 16:(c + 1) * 16],
                PQ[:, c:c + 1], None, ALU.mult), "sv")
        CP("wkab")
        softmax_part(XCHK, "k_")
        v_pool(XCHK, MK, "k_")
        v_big(XCHK, "k_")
        v_merge(XCHK, PEERK, "k_", PMK)
        VW()
        E("v", lambda v: v.tensor_mul(PK[:, :], PQ[:, :], PMK[:, :]), "sv")
        VW()
        for c in range(NCH):
            E("v", lambda v, c=c: v.tensor_scalar(
                WV(c, 0, 2048), MQ(c, 0, 2048), PK[:, c:c + 1], None, ALU.mult),
              "sv")
        CP("wv")


        def s_rstd_group(g):
            WCP("s", f"st{g}", "sv")
            var4 = bass.AP(MV, 8 * g + 1, [[2 * TCH, P], [2, 4]])
            E("s", lambda s, var4=var4: s.activation(var4, var4, AF.Ln,
                                                     bias=CSTS[:, 1:2]), "ss")
            E("s", lambda s, g=g, var4=var4: s.activation(
                RSTD[:, 4 * g:4 * g + 4], var4, AF.Exp, scale=CSTS[:, 0:1]),
              "ss")
            CP(f"rstd{g}")

        for p in range(4):
            xf_load(p)
        s_rstd_group(0)
        s_rstd_group(1)
        for p in range(4, 8):
            xf_load(p)

        def s_late_ln(tid):
            if tid == 8:
                s_rstd_group(2)
            if tid == 16:
                s_rstd_group(3)
            if tid == 20:
                WCP("s", f"xn{TCH - 1}", "sv")
                E("s", lambda s: s.dma_start(OSB[0:1, 0:D], bt_h[:, :]),
                  "b32", 16)
                E("s", lambda s: s.dma_start(OSB[0:1, D:2 * D], bb_h[:, :]),
                  "b32", 16)
                CP("b32")

        def proj_tile(ws, m, movf, mov_a, dst_ap, bias_ap, pool=4):
            bi = cnt["mmA"] % pool
            bank = psA[bi] if bi < 4 else psB[:, 512 * (bi - 4):512 * (bi - 4) + 512]
            if cnt["mmA"] >= pool:
                W("t", "evA", cnt["mmA"] - (pool - 1))
            for k in range(NCH):
                E("t", lambda t, ws=ws, k=k, m=m, bank=bank:
                  t.matmul(bank[:, :] if not isinstance(bank, bass.AP) else bank,
                           WSL(ws, k, m * 128, (m + 1) * 128),
                           movf(k, mov_a, mov_a + 512),
                           start=(k == 0), stop=(k == NCH - 1)),
                  "mmA" if k == NCH - 1 else None)
            tid = cnt["mmA"]
            W("s", "mmA", tid)
            if tid == 1:
                W("s", "bqk", 32)
            s_late_ln(tid)
            bank_ap = bank if isinstance(bank, bass.AP) else bank[:, :]
            if bias_ap is None:
                E("s", lambda s, dst_ap=dst_ap, bank_ap=bank_ap:
                  s.activation(dst_ap, bank_ap, AF.Identity), "evA")
            else:
                E("s", lambda s, dst_ap=dst_ap, bank_ap=bank_ap, bias_ap=bias_ap:
                  s.activation(dst_ap, bank_ap, AF.Identity, bias=bias_ap),
                  "evA")

        W("t", "wq", 16)
        for wi in range(4):
            WCP("t", f"xn{wi}", "sv")
            for wj in range(10):
                E("t", lambda t: t.matmul(psB[:, 0:512], WS0[:, 0:128],
                                          WS0[:, 0:512], start=True, stop=True),
                  None)
        for wi in range(3):
            W("t", "tr0", 16 * (wi + 1))
            for wj in range(8):
                E("t", lambda t: t.matmul(psB[:, 0:512], WS0[:, 0:128],
                                          WS0[:, 0:512], start=True, stop=True),
                  None)

        for n in range(NT):
            W("t", f"tr{n}", 16 * 4)
            W("t", "wq", 16)
            for m in range(NCH):
                proj_tile(WS0, m, XNT, 512 * n, MQ(m, 512 * n, 512 * n + 512),
                          BQ[:, m:m + 1], pool=8)
        CP("step1")

        def mk_tiles(n_list, hook=None):
            W("t", "wk", 16)
            ti = 0
            for n in n_list:
                W("t", f"tr{n}", 16 * 4)
                for m in range(NCH):
                    proj_tile(WS1, m, XNT, 512 * n,
                              MK(m, 512 * n, 512 * n + 512), BK[:, m:m + 1])
                    if hook is not None:
                        hook(ti)
                    ti += 1

        def score_phase(stat_buf, movf, bias_sc, cpbase, ev_gate):
            for n in range(NT):
                if ev_gate is not None:
                    W("t", "evA", ev_gate(n))
                if cnt["mmB"] >= 4:
                    W("t", "evB", cnt["mmB"] - 3)
                for k in range(NCH):
                    E("t", lambda t, n=n, k=k, stat_buf=stat_buf:
                      t.matmul(psB[0:16, 512 * n:512 * n + 512],
                               stat_buf[:, k * 16:(k + 1) * 16],
                               movf(k, 512 * n, 512 * n + 512),
                               start=(k == 0), stop=(k == NCH - 1)),
                      "mmB" if k == NCH - 1 else None)
            W("s", "mmB", cnt["mmB"])
            WCP("s", "prep", "sv")
            WCP("s", f"xn{TCH - 1}", "sv")
            E("s", lambda s, bias_sc=bias_sc: s.activation(
                QSv, psB[0:16, :], AF.Identity, bias=bias_sc,
                scale=CSTS[0:16, 2:3]), "ss")
            E("s", lambda s: s.nop(), "evB", 4)
            CP(cpbase + "s")
            WCP("s", cpbase + "negm", "sv")
            if cpbase == "q_":
                W("s", "bd", 32)
            if cpbase == "k_":
                if not dry:
                    W("s", "mmB", cps["q_erep_done"]["mmB"])
            E("s", lambda s: s.activation(EB[:, :], QSv, AF.Exp,
                                          bias=NEGM[:, :]), "ss")
            CP(cpbase + "eb")

        def erep_group(cpname, c):
            if c == 0:
                WCP("t", cpname + "eb", "ss")
                CP(cpname + "erep0")
            for n in range(NT):
                sl = cnt["mmB"] % 4
                if cnt["mmB"] >= 4:
                    W("t", "evB", cnt["mmB"] - 3)
                E("t", lambda t, c=c, n=n, sl=sl:
                  t.matmul(psB[:, 512 * sl:512 * sl + 512],
                           SELP[:, c * 128:(c + 1) * 128],
                           EB[0:16, 512 * n:512 * n + 512],
                           start=True, stop=True), "mmB")
            if c == NCH - 1:
                CP(cpname + "erep_done")

        def rep_mms(cpname):
            WCP("t", cpname + "bab", "sv")
            sl = cnt["mmB"] % 4
            if cnt["mmB"] >= 4:
                W("t", "evB", cnt["mmB"] - 3)
            E("t", lambda t, sl=sl: t.matmul(psB[:, 512 * sl:512 * sl + 8],
                                             A16[:, :], BAB[:, 0:8],
                                             start=True, stop=True), None)
            E("t", lambda t, sl=sl: t.matmul(psB[:, 512 * sl + 8:512 * sl + 16],
                                             A16[:, :], BAB[:, 8:16],
                                             start=True, stop=True), "mmB")
            CP(cpname + "rep")

        def s_fab(cpname, xch, peer):
            WCP("s", cpname + "negM", "sv")
            E("s", lambda s, xch=xch: s.activation(MRG[:, 2:3], xch[0:16, 8:9],
                                                   AF.Exp, bias=MRG[0:16, 1:2]),
              "ss")
            E("s", lambda s, peer=peer: s.activation(MRG[:, 3:4], peer[0:16, 8:9],
                                                     AF.Exp, bias=MRG[0:16, 1:2]),
              "ss")
            CP(cpname + "fab")

        def h1_phase(n):
            W("t", "fw", 16)
            for m in range(NCH):
                proj_tile(WS0, m, XNT, 512 * n, H1V(m, 0, 512), None)

        def h2_phase(n, first, hook=None):
            W("t", "a1", 16)
            W("t", "evA", cnt["mmA"])
            for m in range(NCH):
                bank = psA[cnt["mmA"] % 4]
                if cnt["mmA"] >= 4:
                    W("t", "evA", cnt["mmA"] - 3)
                for k in range(NCH):
                    E("t", lambda t, k=k, m=m, bank=bank:
                      t.matmul(bank[:, :], WSL(WS1, k, m * 128, (m + 1) * 128),
                               H1V(k, 0, 512),
                               start=(k == 0), stop=(k == NCH - 1)),
                      "mmA" if k == NCH - 1 else None)
                tid = cnt["mmA"]
                W("s", "mmA", tid)
                if first and m == 0:
                    for bn in range(4):
                        W("s", f"tr{bn}", 16 * 4)
                E("s", lambda s, m=m, n=n, bank=bank: s.activation(
                    H2V(m, 512 * n, 512 * n + 512), bank[:, :], AF.Identity),
                  "evA")
                if hook is not None:
                    hook(m)

        W("t", "evA", 32)
        WCP("t", "consts", "cst")
        score_phase(WQAB, MQ, BQAS[:, :], "q_", None)
        mk_tiles([0])
        mk_tiles([1], hook=lambda ti: erep_group("q_", ti) if ti < 8 else None)
        mk_tiles([2, 3])
        h1_phase(0)
        s_fab("q_", XCHQ, PEERQ)
        rep_mms("q_")
        W("t", "evA", 64)
        WCP("t", "wkab", "sv")
        score_phase(WKAB, MK, BKAS[:, :], "k_", None)
        h2_phase(0, True)
        h1_phase(1)
        h2_phase(1, False,
                 hook=lambda m: erep_group("k_", m) if m < 8 else None)
        h1_phase(2)
        h2_phase(2, False)
        s_fab("k_", XCHK, PEERK)
        h1_phase(3)
        rep_mms("k_")
        h2_phase(3, False)
        CP("h2done")

        W("t", "bd", 32)
        bld = []
        for n in range(4):
            src_hl, j = (BTHL, BBHL)[n // 2], n % 2
            bank = psA[cnt["mmA"] % 4]
            if cnt["mmA"] >= 4:
                W("t", "evA", cnt["mmA"] - 3)
            E("t", lambda t, src_hl=src_hl, j=j, bank=bank:
              t.matmul(bank[:, :], ONES2[:, :], src_hl[:, 512 * j:512 * j + 512],
                       start=True, stop=True), "mmA")
            bld.append(cnt["mmA"])
        for n in range(4):
            bank = psA[(bld[n] - 1) % 4]
            W("v", "mmA", bld[n])
            E("v", lambda v, n=n, bank=bank: v.tensor_copy(
                WS1F[:, 512 * n:512 * n + 512], bank[:, :]), "sv")
            E("v", lambda v: v.nop(), "evA", 1)

        def x2_load(i):
            if i >= 2:
                WCP("g", f"fin_u1_{i - 2}", "sv")
            else:
                WCP("g", "k_eb", "ss")
                WCP("g", "k_sum", "sv")
            E("g", lambda g, i=i: g.dma_start(XBv(i, 0, D),
                                              x_h[128 * i:128 * (i + 1), :]),
              "x2e" if i % 2 == 0 else "x2o", 16)

        x2_load(0)
        x2_load(1)

        W("t", "bd", 32)
        WCP("t", "wv", "sv")
        for i in range(TCH):
            tids = []
            for j in range(2):
                if i == 0:
                    W("t", "wt", 16 * (j + 1))
                bank = psA[cnt["mmA"] % 4]
                if cnt["mmA"] >= 4:
                    W("t", "evA", cnt["mmA"] - 3)
                for k in range(NCH):
                    E("t", lambda t, i=i, j=j, k=k, bank=bank:
                      t.matmul(bank[:, :], WV(k, 128 * i, 128 * i + 128),
                               WTV(k, 512 * j, 512 * j + 512),
                               start=(k == 0), stop=False), None)
                for q in range(4):
                    c = 4 * j + q
                    E("t", lambda t, i=i, c=c, q=q, bank=bank:
                      t.matmul(bank[:, 128 * q:128 * q + 128],
                               MQ(c, 128 * i, 128 * i + 128), EYE[:, :],
                               start=False, stop=(q == 3)),
                      "mmA" if q == 3 else None)
                tids.append(cnt["mmA"])
            for j in range(2):
                if i == 0:
                    W("t", "a2", 16 * (j + 1))
                bank = psA[cnt["mmA"] % 4]
                if cnt["mmA"] >= 4:
                    W("t", "evA", cnt["mmA"] - 3)
                for k in range(NCH):
                    E("t", lambda t, i=i, j=j, k=k, bank=bank:
                      t.matmul(bank[:, :], H2V(k, 128 * i, 128 * i + 128),
                               WSL(WS0, k, 512 * j, 512 * j + 512),
                               start=(k == 0), stop=(k == NCH - 1)),
                      "mmA" if k == NCH - 1 else None)
                tids.append(cnt["mmA"])

            FBV = [WS1F[:, 2048 + 512 * jj:2560 + 512 * jj] for jj in range(2)]
            U1B = [WS1F[:, 3072 + 512 * jj:3584 + 512 * jj] for jj in range(2)]
            for j in range(2):
                W("v", "mmA", tids[j])
                if j == 0:
                    W("v", "x2e" if i % 2 == 0 else "x2o", 16 * (i // 2 + 1))
                    if i >= 2:
                        W("v", "oe" if i % 2 == 0 else "oo",
                          16 * ((i - 2) // 2 + 1))
                    if i == 0:
                        VW()
                E("v", lambda v, i=i, j=j, a_bank=psA[(tids[j] - 1) % 4]:
                  v.tensor_add(U1B[j], a_bank[:, :],
                               XBv(i, 512 * j, 512 * j + 512)), "sv")
                E("v", lambda v: v.nop(), "evA", 1)
            CP(f"fin_u1_{i}")
            for j in range(2):
                W("v", "mmA", tids[2 + j])
                E("v", lambda v, j=j, f_bank=psA[(tids[2 + j] - 1) % 4]:
                  v.tensor_add(FBV[j], f_bank[:, :],
                               WS1F[:, 1024 + 512 * j:1536 + 512 * j]), "sv")
                E("v", lambda v: v.nop(), "evA", 1)
                CP(f"fin_fb_{i}_{j}")

            for j in range(2):
                WCP("s", f"fin_fb_{i}_{j}", "sv")
                if i >= 1:
                    WCP("s", f"fin_og_{i - 1}_{j}", "sv")
                E("s", lambda s, j=j: s.activation(
                    T2[j][:, :], FBV[j][:, :], AF.Tanh), "ss")
                CP(f"fin_t2_{i}_{j}")

            VW()
            for j in range(2):
                E("v", lambda v, j=j: v.tensor_add(
                    U1B[j], U1B[j], WS1F[:, 512 * j:512 * j + 512]), "sv")
            for j in range(2):
                WCP("v", f"fin_t2_{i}_{j}", "ss")
                VW()
                E("v", lambda v, i=i, j=j: v.tensor_add(
                    OSB[:, (i % 2) * D + 512 * j:(i % 2) * D + 512 * j + 512],
                    U1B[j], T2[j][:, :]), "sv")
                CP(f"fin_og_{i}_{j}")
            CP(f"fin_{i}")

            if i % 2 == 0:
                WCP("s", f"fin_{i}", "sv")
                E("s", lambda s, i=i: s.dma_start(
                    out_h[128 * i:128 * (i + 1), :], OS[i % 2]), "oe", 16)
            if i + 2 < TCH:
                x2_load(i + 2)

        W("s", "oe", 16 * (TCH // 2))
        W("s", "oo", 16 * (TCH // 2))
        return cnt

    final_counts = walk("dry", None, True)
    for nm, snap in cps.items():
        for k, v in snap.items():
            assert v <= final_counts[k], (nm, k, v, final_counts[k])

    with nc.Block() as block:
        @block.gpsimd
        def _(g):
            walk("g", g, False)

        @block.sync
        def _(sy):
            walk("sy", sy, False)

        @block.vector
        def _(v):
            walk("v", v, False)

        @block.scalar
        def _(s):
            walk("s", s, False)

        @block.tensor
        def _(t):
            walk("t", t, False)

    es.close()
    return nc


def make_in_maps(g):
    x = np.asarray(g["L0_attn_hidden_states"], dtype=np.float32)
    Wq = np.asarray(g["L0_attn_Wq"], np.float32)
    Wk = np.asarray(g["L0_attn_Wk"], np.float32)
    Wt = np.asarray(g["L0_attn_Wt"], np.float32)
    FW = np.asarray(g["L0_ffn_weight"], np.float32)
    A1 = np.asarray(g["L0_ffn_aux_weight1"], np.float32)
    A2 = np.asarray(g["L0_ffn_aux_weight2"], np.float32)
    Wqa = np.asarray(g["L0_attn_Wqa"], np.float32)
    Wka = np.asarray(g["L0_attn_Wka"], np.float32)
    bq = np.asarray(g["L0_attn_bq"], np.float32)
    bk = np.asarray(g["L0_attn_bk"], np.float32)
    bqa = np.asarray(g["L0_attn_bqa"], np.float32)
    bka = np.asarray(g["L0_attn_bka"], np.float32)
    bt = np.asarray(g["L0_attn_bt"], np.float32)
    bb = np.asarray(g["L0_ffn_bias"], np.float32)

    C = np.ascontiguousarray
    wqa_l = C(Wqa.reshape(8, 128, 16).transpose(1, 0, 2).reshape(128, 128))
    wka_l = C(Wka.reshape(8, 128, 16).transpose(1, 0, 2).reshape(128, 128))
    bq_l = C(bq.reshape(8, 128).T)
    bk_l = C(bk.reshape(8, 128).T)
    bqa_l = C(bqa.reshape(16, 1))
    bka_l = C(bka.reshape(16, 1))
    bt_l = C(bt.reshape(1, 1024))
    bb_l = C(bb.reshape(1, 1024))

    selp = np.zeros((16, 8, 128), np.float32)
    for c in range(8):
        selp[2 * c, c, 0:64] = 1.0
        selp[2 * c + 1, c, 64:128] = 1.0
    selp = C(selp.reshape(16, 1024).astype(BF))
    a16 = np.zeros((16, 128), np.float32)
    a16[0::2, 0:64] = 1.0
    a16[1::2, 64:128] = 1.0
    a16 = C(a16.astype(BF))
    maskb = np.zeros((16, 8), np.float32)
    for h in range(16):
        maskb[h, h // 2] = 1.0
    eye = C(np.eye(128, dtype=np.float32).astype(BF))
    ones2 = C(np.ones((2, 128), np.float32).astype(BF))

    in_maps = []
    for c in range(8):
        b, hf = c // 2, c % 2
        slab = C(x[b, hf * 2048:(hf + 1) * 2048, :])
        slot1 = np.zeros((128, 8), np.float32); slot1[:, c] = 1.0
        peer1 = np.zeros((128, 8), np.float32); peer1[:, c ^ 1] = 1.0
        in_maps.append({
            "x": slab, "Wq": Wq, "Wk": Wk, "Wt": Wt, "FW": FW, "A1": A1,
            "A2": A2, "WqaL": wqa_l, "WkaL": wka_l, "bqL": bq_l, "bkL": bk_l,
            "bqaL": bqa_l, "bkaL": bka_l, "btL": bt_l, "bbL": bb_l,
            "selp": selp, "a16": a16, "maskb": maskb, "eye": eye,
            "ones2": ones2, "slot1": slot1, "peer1": peer1,
        })
    return in_maps


def kernel(**inputs):
    g = {k: np.asarray(v) for k, v in inputs.items()}
    in_maps = make_in_maps(g)
    nc = build_nc()
    res = run_bass_kernel_spmd(nc, in_maps, core_ids=list(range(8)))
    out = np.empty((4, 4096, 1024), np.float32)
    for c in range(8):
        b, hf = c // 2, c % 2
        out[b, hf * 2048:(hf + 1) * 2048, :] = res.results[c]["out"]
    return out
